# version 18
# baseline (speedup 1.0000x reference)
import numpy as np
import ml_dtypes
from contextlib import ExitStack
from types import SimpleNamespace

import concourse.bass as bass
import concourse.mybir as mybir
from concourse.bass import IndirectOffsetOnAxis
from concourse.bass_utils import run_bass_kernel_spmd

F32 = mybir.dt.float32
BF16 = mybir.dt.bfloat16
AF = mybir.ActivationFunctionType
ALU = mybir.AluOpType
AX = mybir.AxisListType

NCORE = 8
D = 1024
TPC = 2048
NTB = 4
TBW = 512
NDC = 8
NEXP = 8
HID = 4096
NHC = 32
CAP = 5120.0
EPS_LN = 1e-5
EPS_ATT = 1e-6
EPS_FEAT = 1e-6
PNORM = 1.0 / (64.0 ** 0.25)

ENGINES = ("sync", "gpsimd", "tensor", "vector", "scalar")

_TAPS = ()
_CKPT = 99


class K:

    def __init__(self, nc, stk):
        self.nc = nc
        self.stk = stk
        self.seq = []
        self.ctrs = []
        self.veS = self.ctr("veS")
        self.scS = self.ctr("scS")
        self.mmS = self.ctr("mmS")
        self.bw = [None] * 8
        self.br = [[] for _ in range(8)]

    def ctr(self, name):
        h = self.stk.enter_context(self.nc.semaphore(name))
        c = SimpleNamespace(h=h, n=0, name=name)
        self.ctrs.append(c)
        return c

    def emit(self, eng, fn, waits=(), incs=()):
        seen = {}
        for c, v in waits:
            assert 0 <= v <= c.n, f"wait {c.name}>={v} but only {c.n} emitted"
            if v > 0:
                prev = seen.get(id(c))
                seen[id(c)] = (c.h, v if prev is None else max(v, prev[1]))
        ws = list(seen.values())
        rincs = []
        for c, k in incs:
            c.n += k
            rincs.append((c.h, k))
        self.seq.append((eng, ws, fn, rincs))

    def v(self, fn, waits=()):
        w = list(waits)
        if self.veS.n:
            w.append((self.veS, self.veS.n))
        self.emit("vector", fn, w, [(self.veS, 1)])
        return (self.veS, self.veS.n)

    def s(self, fn, waits=()):
        w = list(waits)
        if self.scS.n:
            w.append((self.scS, self.scS.n))
        self.emit("scalar", fn, w, [(self.scS, 1)])
        return (self.scS, self.scS.n)

    def dma(self, eng, fn, c, waits=()):
        self.emit(eng, fn, waits, [(c, 16)])
        return (c, c.n)

    def g(self, fn, waits=(), incs=()):
        self.emit("gpsimd", fn, waits, incs)

    def mm(self, bank, fn, waits=()):
        w = list(waits) + self.br[bank]
        if self.bw[bank] is not None:
            w.append(self.bw[bank])
        self.emit("tensor", fn, w, [(self.mmS, 1)])
        tok = (self.mmS, self.mmS.n)
        self.bw[bank] = tok
        self.br[bank] = []
        return tok

    def rd(self, eng, bank, fn, waits=()):
        w = list(waits)
        if self.bw[bank] is not None:
            w.append(self.bw[bank])
        tok = self.v(fn, w) if eng == "vector" else self.s(fn, w)
        self.br[bank].append(tok)
        return tok

    def snap(self):
        return [(c, c.n) for c in self.ctrs]

    def barrier_all(self, exclude=()):
        ex = set(id(c) for c in exclude)
        sn = [(c, v) for c, v in self.snap() if id(c) not in ex]
        for e in ENGINES:
            self.emit(e, None, waits=[(c, v) for c, v in sn if v > 0])
        self.bw = [None] * 8
        self.br = [[] for _ in range(8)]

    def flush(self):
        with self.nc.Block(no_gpsimd_drain=True) as block:
            for en in ENGINES:
                def run(engine, en=en):
                    for eng, ws, fn, incs in self.seq:
                        if eng != en:
                            continue
                        for h, v in ws:
                            engine.wait_ge(h, v)
                        inst = fn(engine) if fn is not None else None
                        for h, kk in incs:
                            inst.then_inc(h, kk)
                getattr(block, en)(run)
        self.seq = []


def build_program():
    nc = bass.Bass("TRN2", target_bir_lowering=False)

    def inp(name, shape, dt=F32):
        return nc.dram_tensor(name, list(shape), dt, kind="ExternalInput").ap()

    xfm = inp("xfm", (NDC, NTB, 128, TBW))
    wqT = inp("wqT", (D, D))
    wkT = inp("wkT", (D, D))
    wvT = inp("wvT", (D, D))
    woT = inp("woT", (D, D))
    bqkvo = inp("bqkvo", (128, 32))
    bvrow = inp("bvrow", (1, D))
    ptT = inp("ptT", (64, 64))
    gw = inp("gw", (D, NEXP))
    myw1 = inp("myw1", (D, HID))
    myfcb = inp("myfcb", (128, 32))
    myw2 = inp("myw2", (HID, D))
    mypjbr = inp("mypjbr", (1, D))
    myrows = inp("myrows", (16, 1), mybir.dt.int32)
    cmself = inp("cmself", (8, 8))
    cmprev = inp("cmprev", (8, 8))

    identf_t = nc.inline_tensor(np.eye(128, dtype=np.float32), "identc")
    identb_t = nc.inline_tensor(np.eye(128, dtype=ml_dtypes.bfloat16), "identbc")
    gall_np = (np.arange(128)[:, None] + 128 * np.arange(128)[None, :]).astype(np.float32)
    gall_t = nc.inline_tensor(gall_np, "gallc")

    def intern(name, shape, dt=F32):
        kind = "ExternalOutput" if name in _TAPS else "Internal"
        return nc.dram_tensor(name, list(shape), dt, kind=kind).ap()

    x1d = intern("x1d", (NDC, NTB, 128, TBW))
    xn2t = intern("xn2t", (TPC, D), BF16)
    xgall = intern("xgall", (NCORE * TPC, D), BF16)
    swpd = intern("swpd", (16, 2048))
    swall = intern("swall", (128, 2048))
    swd = intern("swd", (5248, 2))
    ygl = intern("ygl", (NCORE * TPC, D))
    yred = intern("yred", (TPC, D))

    kvin = intern("kvin", (128, 520))
    kvout = intern("kvout", (128, 520))
    ccin = intern("ccin", (16, 8))
    ccout = intern("ccout", (16, 8))
    ysfm = nc.dram_tensor("ysfm", [NDC, NTB, 128, TBW], F32, kind="ExternalOutput").ap()


    stk = ExitStack()
    with stk:
        E = stk.enter_context
        P = K(nc, stk)

        qC = P.ctr("qC")
        qX = [P.ctr(f"qX{i}") for i in range(4)]
        qWT = P.ctr("qWT")
        qKV = P.ctr("qKV")
        qST = [P.ctr(f"qST{i}") for i in range(2)]
        qN = [P.ctr(f"qN{i}") for i in range(2)]
        qCC = P.ctr("qCC")
        qFW = P.ctr("qFW")
        qFX = [P.ctr(f"qFX{i}") for i in range(2)]
        qYP = [P.ctr(f"qYP{i}") for i in range(2)]
        qFS = [P.ctr(f"qFS{i}") for i in range(2)]
        qOUT = P.ctr("qOUT")
        ccS = P.ctr("ccS")
        qTAP = P.ctr("qTAP")

        def tap(name, buf, rows, cols, dt=F32):
            if name not in _TAPS:
                return
            t = nc.dram_tensor(name, [rows, cols], dt, kind="ExternalOutput").ap()
            P.barrier_all()
            P.dma("sync", lambda e: e.dma_start(out=t, in_=buf[0:rows, 0:cols]), qTAP)
            P.barrier_all()

        onesC = E(nc.sbuf_tensor("onesC", [128, 128], F32))
        fcbs = E(nc.sbuf_tensor("fcbs", [128, 32], F32))
        pb = [E(nc.psum_tensor(f"pb{i}", [128, 512], F32)) for i in range(6)]
        pb = pb + [None, None]

        P.dma("sync", lambda e: e.dma_start(out=fcbs[:], in_=myfcb), qC)
        P.v(lambda e: e.memset(onesC[:], 1.0))

        def ln_stats(src, xbuf, sqbuf, arena):
            for tbg in range(2):
                if tbg == 0:
                    slot_use = [None] * 4
                    sq_use = [None, None]
                else:
                    prev_mm = (P.mmS, P.mmS.n)
                    prev_ve = (P.veS, P.veS.n)
                    slot_use = [prev_mm] * 4
                    sq_use = [prev_ve, prev_ve]
                for dc in range(NDC):
                    for tb2 in range(2):
                        tb = tbg * 2 + tb2
                        bs, bq = tb2, 2 + tb2
                        i = dc * 2 + tb2
                        sl = i % 4
                        w = [slot_use[sl]] if slot_use[sl] else []
                        ld = P.dma("sync", lambda e, dc=dc, tb=tb, sl=sl: e.dma_start(
                            out=xbuf[:, sl * 512:(sl + 1) * 512],
                            in_=src[dc, tb]), qX[sl], w)
                        sq = P.s(lambda e, sl=sl: e.activation(
                            sqbuf[:, (sl % 2) * 512:(sl % 2 + 1) * 512],
                            xbuf[:, sl * 512:(sl + 1) * 512], AF.Square),
                            [ld] + ([sq_use[sl % 2]] if sq_use[sl % 2] else []))
                        P.mm(bs, lambda e, bs=bs, sl=sl, dc=dc: e.matmul(
                            pb[bs][0:33, :], onesC[:, 0:33], xbuf[:, sl * 512:(sl + 1) * 512],
                            start=(dc == 0), stop=(dc == NDC - 1)), [ld])
                        tok = P.mm(bq, lambda e, bq=bq, sl=sl, dc=dc: e.matmul(
                            pb[bq][0:33, :], onesC[:, 0:33],
                            sqbuf[:, (sl % 2) * 512:(sl % 2 + 1) * 512],
                            start=(dc == 0), stop=(dc == NDC - 1)), [sq])
                        slot_use[sl] = tok
                        sq_use[sl % 2] = tok
                stats_done = (P.mmS, P.mmS.n)
                for tb2 in range(2):
                    tb = tbg * 2 + tb2
                    bs, bq = tb2, 2 + tb2
                    cl = slice(tb * TBW, (tb + 1) * TBW)
                    P.rd("vector", bs, lambda e, bs=bs, cl=cl: e.tensor_scalar(
                        arena[0:1, cl], pb[bs][0:1, :], 1.0 / D, None, ALU.mult), [stats_done])
                    P.rd("vector", bs, lambda e, bs=bs: e.tensor_scalar(
                        sqbuf[32:33, 0:512], pb[bs][32:33, :], 1.0 / D, None, ALU.mult))
                    P.v(lambda e: e.tensor_tensor(sqbuf[32:33, 0:512], sqbuf[32:33, 0:512],
                                                  sqbuf[32:33, 0:512], ALU.mult))
                    P.rd("vector", bq, lambda e, bq=bq: e.tensor_scalar(
                        sqbuf[32:33, 512:1024], pb[bq][32:33, :], 1.0 / D, EPS_LN,
                        ALU.mult, ALU.add))
                    P.v(lambda e: e.tensor_tensor(sqbuf[32:33, 512:1024], sqbuf[32:33, 512:1024],
                                                  sqbuf[32:33, 0:512], ALU.subtract))
                    P.s(lambda e: e.activation(sqbuf[32:33, 0:512], sqbuf[32:33, 512:1024],
                                               AF.Sqrt), [(P.veS, P.veS.n)])
                    P.v(lambda e, cl=cl: e.reciprocal(arena[32:33, cl], sqbuf[32:33, 0:512]),
                        [(P.scS, P.scS.n)])

        with ExitStack() as s1:
            EA = s1.enter_context
            pb[6] = EA(nc.psum_tensor("pb6f", [128, 512], F32))
            pb[7] = EA(nc.psum_tensor("pb7f", [128, 512], F32))
            xn = EA(nc.sbuf_tensor("xn", [128, NDC * TPC], BF16))
            qf = EA(nc.sbuf_tensor("qf", [128, NDC * TPC], BF16))
            kf = EA(nc.sbuf_tensor("kf", [128, 16 * 1024], BF16))
            vt = EA(nc.sbuf_tensor("vt", [128, 16 * 1024], BF16))
            qk = EA(nc.sbuf_tensor("qk", [128, TPC], BF16))
            xst = EA(nc.sbuf_tensor("xst", [128, 4 * 512], F32))
            wts = EA(nc.sbuf_tensor("wts", [128, 8 * 1024], BF16))
            scr = EA(nc.sbuf_tensor("scr", [128, 3 * 512], F32))
            rbsb = EA(nc.sbuf_tensor("rbsb", [128, 512], F32))
            stg = EA(nc.sbuf_tensor("stg", [128, 2 * 512], F32))
            kvpack = EA(nc.sbuf_tensor("kvpack", [128, 520], F32))
            kv_bf = EA(nc.sbuf_tensor("kv_bf", [128, 520], BF16))
            blkP = EA(nc.sbuf_tensor("blkP", [128, 128], BF16))
            bqs = EA(nc.sbuf_tensor("bqs", [128, 32], F32))
            bvb = EA(nc.sbuf_tensor("bvb", [128, D], F32))
            ptsb = EA(nc.sbuf_tensor("ptsb", [128, 64], F32))
            ones_bf = EA(nc.sbuf_tensor("ones_bf", [128, 1], BF16))
            arena = EA(nc.sbuf_tensor("arena", [48, 2048], F32))
            P.v(lambda e: e.memset(arena[:], 0.0))

            P.dma("sync", lambda e: e.dma_start(out=bqs[:], in_=bqkvo), qC)
            P.dma("sync", lambda e: e.dma_start(out=ptsb[0:64, :], in_=ptT), qC)
            P.dma("sync", lambda e: e.dma_start(out=ptsb[64:128, :], in_=ptT), qC)
            P.dma("sync", lambda e: e.dma_start(out=bvb[0:1, :], in_=bvrow), qC)
            cload = (qC, qC.n)
            P.v(lambda e: e.memset(ones_bf[:], 1.0))
            mz = P.v(lambda e: e.memset(blkP[:], 0.0))
            P.s(lambda e: e.activation(blkP[0:64, 0:64], ptsb[0:64, :], AF.Copy, scale=PNORM),
                [cload, mz])
            blkP_rdy = P.s(lambda e: e.activation(blkP[64:128, 64:128], ptsb[64:128, :],
                                                  AF.Copy, scale=PNORM), [cload])
            vz = (P.veS, P.veS.n)
            P.mm(6, lambda e: e.matmul(pb[6][:], onesC[0:1, :], bvb[0:1, 0:512],
                                       start=True, stop=True), [cload, vz])
            P.mm(7, lambda e: e.matmul(pb[7][:], onesC[0:1, :], bvb[0:1, 512:1024],
                                       start=True, stop=True), [cload, vz])
            P.rd("vector", 6, lambda e: e.tensor_copy(bvb[:, 0:512], pb[6][:]))
            bvb_rdy = P.rd("vector", 7, lambda e: e.tensor_copy(bvb[:, 512:1024], pb[7][:]))

            def load_wt(src, waits):
                last = None
                for dc in range(NDC):
                    last = P.dma("gpsimd", lambda e, dc=dc, src=src: e.dma_start(
                        out=wts[:, dc * 1024:(dc + 1) * 1024],
                        in_=src[dc * 128:(dc + 1) * 128, :]), qWT,
                        waits if dc == 0 else [])
                return last

            wq_rdy = load_wt(wqT, [])

            P.barrier_all()
            ln_stats(xfm, xst, scr, arena)
            P.barrier_all()
            tap("t_ar", arena, 48, 2048, F32)

            slot_use = [None] * 4
            for tb in range(NTB):
                bm, br_ = 4 + (tb % 2) * 2, 5 + (tb % 2) * 2
                P.mm(bm, lambda e, tb=tb, bm=bm: e.matmul(
                    pb[bm][:], onesC[0:1, :], arena[0:1, tb * TBW:(tb + 1) * TBW],
                    start=True, stop=True))
                P.mm(br_, lambda e, tb=tb, br_=br_: e.matmul(
                    pb[br_][:], onesC[32:33, :], arena[32:33, tb * TBW:(tb + 1) * TBW],
                    start=True, stop=True))
                for dc in range(NDC):
                    i = tb * NDC + dc
                    sl = i % 4
                    w = [slot_use[sl]] if slot_use[sl] else []
                    ld = P.dma("sync", lambda e, dc=dc, tb=tb, sl=sl: e.dma_start(
                        out=xst[:, sl * 512:(sl + 1) * 512],
                        in_=xfm[dc, tb]), qX[sl], w)
                    P.rd("vector", bm, lambda e, sl=sl, bm=bm: e.tensor_tensor(
                        scr[:, (sl % 2) * 512:(sl % 2 + 1) * 512],
                        xst[:, sl * 512:(sl + 1) * 512], pb[bm][:], ALU.subtract), [ld])
                    tok = P.rd("vector", br_, lambda e, sl=sl, dc=dc, tb=tb, br_=br_: e.tensor_tensor(
                        xn[:, dc * TPC + tb * TBW: dc * TPC + (tb + 1) * TBW],
                        scr[:, (sl % 2) * 512:(sl % 2 + 1) * 512], pb[br_][:], ALU.mult))
                    slot_use[sl] = tok
            P.barrier_all()
            tap("t_xn", xn, 128, NDC * TPC, BF16)

            qk_guard = None

            def proj(which, w_rdy, bias_off):
                nonlocal qk_guard
                for oc in range(NDC):
                    banks = [0, 1, 2, 3] if oc % 2 == 0 else [4, 5, 6, 7]
                    for tb in range(NTB):
                        bk = banks[tb]
                        def mmg2(e, oc=oc, tb=tb, bk=bk):
                            last = None
                            for dc in range(NDC):
                                last = e.matmul(
                                    pb[bk][:],
                                    wts[:, dc * 1024 + oc * 128: dc * 1024 + (oc + 1) * 128],
                                    xn[:, dc * TPC + tb * TBW: dc * TPC + (tb + 1) * TBW],
                                    start=(dc == 0), stop=(dc == NDC - 1))
                            return last
                        P.mm(bk, mmg2, [w_rdy])
                    for tb in range(NTB):
                        bk = banks[tb]
                        w = [qk_guard] if qk_guard else []
                        P.rd("scalar", bk, lambda e, oc=oc, tb=tb, bk=bk: e.add(
                            qk[:, tb * TBW:(tb + 1) * TBW], pb[bk][:],
                            bqs[:, bias_off + oc: bias_off + oc + 1]), w)
                    if which == "q":
                        for tb in range(NTB):
                            bk = banks[tb]
                            tok = P.mm(bk, lambda e, tb=tb, bk=bk: e.matmul(
                                pb[bk][:], blkP[:], qk[:, tb * TBW:(tb + 1) * TBW],
                                start=True, stop=True), [blkP_rdy])
                            P.rd("vector", bk, lambda e, oc=oc, tb=tb, bk=bk: e.tensor_scalar(
                                qf[:, oc * TPC + tb * TBW: oc * TPC + (tb + 1) * TBW],
                                pb[bk][:], 0.0, EPS_FEAT, ALU.max, ALU.add))
                        qk_guard = tok
                    else:
                        for tb4 in range(NTB):
                            bk = banks[tb4]
                            def mmk(e, oc=oc, tb4=tb4, bk=bk):
                                last = None
                                for j in range(4):
                                    tc = tb4 * 4 + j
                                    last = e.matmul(
                                        pb[bk][:, j * 128:(j + 1) * 128],
                                        qk[:, tc * 128:(tc + 1) * 128], blkP[:],
                                        start=True, stop=True)
                                return last
                            tok = P.mm(bk, mmk, [blkP_rdy])
                            def evk(e, oc=oc, tb4=tb4, bk=bk):
                                last = None
                                for j in range(4):
                                    tc = tb4 * 4 + j
                                    last = e.tensor_scalar(
                                        kf[:, tc * 1024 + oc * 128: tc * 1024 + (oc + 1) * 128],
                                        pb[bk][:, j * 128:(j + 1) * 128], 0.0, EPS_FEAT,
                                        ALU.max, ALU.add)
                                return last
                            P.rd("vector", bk, evk)
                        qk_guard = tok

            proj("q", wq_rdy, 0)
            wk_rdy = load_wt(wkT, [(P.mmS, P.mmS.n)])
            proj("k", wk_rdy, 8)
            wv_rdy = load_wt(wvT, [(P.mmS, P.mmS.n)])

            for tc in range(16):
                for half in range(2):
                    bk = 4 * (tc % 2) + half
                    def mmv(e, tc=tc, half=half, bk=bk):
                        last = None
                        for dc in range(NDC):
                            last = e.matmul(
                                pb[bk][:],
                                xn[:, dc * TPC + tc * 128: dc * TPC + (tc + 1) * 128],
                                wts[:, dc * 1024 + half * 512: dc * 1024 + (half + 1) * 512],
                                start=(dc == 0), stop=(dc == NDC - 1))
                        return last
                    P.mm(bk, mmv, [wv_rdy])
                    P.rd("vector", bk, lambda e, tc=tc, half=half, bk=bk: e.scalar_tensor_tensor(
                        vt[:, tc * 1024 + half * 512: tc * 1024 + (half + 1) * 512],
                        pb[bk][:], 1.0, bvb[:, half * 512:(half + 1) * 512],
                        ALU.mult, ALU.add), [bvb_rdy])
            wo_rdy = load_wt(woT, [(P.mmS, P.mmS.n)])
            P.barrier_all()
            tap("t_qf", qf, 128, NDC * TPC, BF16)
            tap("t_kf", kf, 128, 16 * 1024, BF16)
            tap("t_vt", vt, 128, 16 * 1024, BF16)

            for hp in range(NDC):
                bk = hp % 2
                for par in range(2):
                    base = par * 64
                    h_ = hp * 2 + par
                    def mmkv(e, h_=h_, base=base, bk=bk):
                        last = None
                        for tc in range(16):
                            last = e.matmul(
                                pb[bk][base:base + 64, 0:64],
                                kf[:, tc * 1024 + h_ * 64: tc * 1024 + (h_ + 1) * 64],
                                vt[:, tc * 1024 + h_ * 64: tc * 1024 + (h_ + 1) * 64],
                                start=(tc == 0), stop=(tc == 15))
                        return last
                    P.mm(bk, mmkv)
                def mmks(e, hp=hp):
                    last = None
                    for tc in range(16):
                        last = e.matmul(
                            pb[2][:, hp:hp + 1],
                            kf[:, tc * 1024 + hp * 128: tc * 1024 + (hp + 1) * 128],
                            ones_bf[:], start=(tc == 0), stop=(tc == 15))
                    return last
                P.mm(2, mmks)
                P.rd("vector", bk, lambda e, hp=hp, bk=bk: e.tensor_copy(
                    kvpack[:, hp * 65: hp * 65 + 64], pb[bk][:, 0:64]))
            P.rd("vector", 2, lambda e: e.tensor_copy(kvpack[:, 64:520:65], pb[2][:, 0:8]))
            kv_done = (P.veS, P.veS.n)
            P.dma("gpsimd", lambda e: e.dma_start(out=kvin, in_=kvpack[:]), qKV, [kv_done])
            P.g(lambda e: e.collective_compute(
                "AllReduce", ALU.add, replica_groups=[[0, 1], [2, 3], [4, 5], [6, 7]],
                ins=[kvin.opt()], outs=[kvout.opt()]), [(qKV, qKV.n)], [(ccS, 1)])
            kvld = P.dma("gpsimd", lambda e: e.dma_start(out=kvpack[:], in_=kvout), qKV, [(ccS, 1)])
            kvbf_rdy = P.v(lambda e: e.tensor_copy(kv_bf[:], kvpack[:]), [kvld])
            P.barrier_all()
            tap("t_kv", kvpack, 128, 520, F32)

            rden_guard = None
            rbsE_guard = None
            rbsO_guard = None
            for hp in range(NDC):
                for tb in range(NTB):
                    i = hp * NTB + tb
                    bkE = i % 2
                    bkO = 2 + i % 2
                    bkD = 4 + i % 2
                    c0 = hp * TPC + tb * TBW
                    P.mm(bkE, lambda e, hp=hp, c0=c0, bkE=bkE: e.matmul(
                        pb[bkE][0:65, :], kv_bf[0:64, hp * 65:(hp + 1) * 65],
                        qf[0:64, c0:c0 + TBW], start=True, stop=True))
                    P.mm(bkO, lambda e, hp=hp, c0=c0, bkO=bkO: e.matmul(
                        pb[bkO][64:128, :], kv_bf[64:128, hp * 65: hp * 65 + 64],
                        qf[64:128, c0:c0 + TBW], start=True, stop=True))
                    P.mm(bkD, lambda e, hp=hp, c0=c0, bkD=bkD: e.matmul(
                        pb[bkD][64:65, :], kv_bf[64:128, hp * 65 + 64: hp * 65 + 65],
                        qf[64:128, c0:c0 + TBW], start=True, stop=True))
                    wrd = [rden_guard] if rden_guard else []
                    P.rd("vector", bkE, lambda e, bkE=bkE: e.tensor_scalar(
                        scr[64:65, 0:512], pb[bkE][64:65, :], EPS_ATT, None, ALU.add), wrd)
                    P.v(lambda e: e.reciprocal(scr[64:65, 0:512], scr[64:65, 0:512]))
                    P.rd("vector", bkD, lambda e, bkD=bkD: e.tensor_scalar(
                        scr[64:65, 512:1024], pb[bkD][64:65, :], EPS_ATT, None, ALU.add))
                    rcp = P.v(lambda e: e.reciprocal(scr[64:65, 512:1024], scr[64:65, 512:1024]))
                    bE = P.mm(6, lambda e: e.matmul(
                        pb[6][0:64, :], onesC[64:65, 0:64], scr[64:65, 0:512],
                        start=True, stop=True), [rcp])
                    bO = P.mm(7, lambda e: e.matmul(
                        pb[7][64:128, :], onesC[64:65, 0:64], scr[64:65, 512:1024],
                        start=True, stop=True), [rcp])
                    rden_guard = bO
                    rbsE = P.rd("scalar", 6, lambda e: e.copy(rbsb[0:64, :], pb[6][0:64, :]),
                                [rbsE_guard] if rbsE_guard else [])
                    rbsO = P.rd("scalar", 7, lambda e: e.copy(rbsb[64:128, :], pb[7][64:128, :]),
                                [rbsO_guard] if rbsO_guard else [])
                    rbsE_guard = P.rd("vector", bkE, lambda e, c0=c0, bkE=bkE: e.tensor_tensor(
                        qf[0:64, c0:c0 + TBW], pb[bkE][0:64, :], rbsb[0:64, :],
                        ALU.mult), [rbsE])
                    rbsO_guard = P.rd("vector", bkO, lambda e, c0=c0, bkO=bkO: e.tensor_tensor(
                        qf[64:128, c0:c0 + TBW], pb[bkO][64:128, :], rbsb[64:128, :],
                        ALU.mult), [rbsO])
            P.barrier_all()
            tap("t_att", qf, 128, NDC * TPC, BF16)

            slot_use = [None] * 4
            store_use = [None, None]
            for oc in range(NDC):
                banks = [0, 1, 2, 3] if oc % 2 == 0 else [4, 5, 6, 7]
                for tb in range(NTB):
                    bk = banks[tb]
                    def mmo(e, oc=oc, tb=tb, bk=bk):
                        last = None
                        for dc in range(NDC):
                            last = e.matmul(
                                pb[bk][:],
                                wts[:, dc * 1024 + oc * 128: dc * 1024 + (oc + 1) * 128],
                                qf[:, dc * TPC + tb * TBW: dc * TPC + (tb + 1) * TBW],
                                start=(dc == 0), stop=(dc == NDC - 1))
                        return last
                    P.mm(bk, mmo, [wo_rdy])
                for tb in range(NTB):
                    bk = banks[tb]
                    i = oc * NTB + tb
                    sl = i % 4
                    ssl = i % 2
                    w = [slot_use[sl]] if slot_use[sl] else []
                    ld = P.dma("sync", lambda e, oc=oc, tb=tb, sl=sl: e.dma_start(
                        out=xst[:, sl * 512:(sl + 1) * 512],
                        in_=xfm[oc, tb]), qX[sl], w)
                    wst = [ld] + ([store_use[ssl]] if store_use[ssl] else [])
                    ev = P.rd("vector", bk, lambda e, oc=oc, sl=sl, ssl=ssl, bk=bk: e.scalar_tensor_tensor(
                        stg[:, ssl * 512:(ssl + 1) * 512], pb[bk][:],
                        bqs[:, 24 + oc: 25 + oc], xst[:, sl * 512:(sl + 1) * 512],
                        ALU.add, ALU.add), wst)
                    slot_use[sl] = ev
                    P.dma("sync", lambda e, oc=oc, tb=tb, ssl=ssl: e.dma_start(
                        out=x1d[oc, tb],
                        in_=stg[:, ssl * 512:(ssl + 1) * 512]), qST[ssl], [ev])
                    store_use[ssl] = P.dma("sync", lambda e, oc=oc, tb=tb, ssl=ssl: e.dma_start(
                        out=ysfm[oc, tb],
                        in_=stg[:, ssl * 512:(ssl + 1) * 512]), qST[ssl], [])
            P.barrier_all()
            P.flush()

        s23 = ExitStack()
        w1s = s23.enter_context(nc.sbuf_tensor("w1s", [128, NDC * HID], BF16))
        identf = s23.enter_context(nc.sbuf_tensor("identf", [128, 128], F32))
        identb = s23.enter_context(nc.sbuf_tensor("identb", [128, 128], BF16))
        pb[6] = s23.enter_context(nc.psum_tensor("pb6b", [128, 1024], BF16))
        pb[7] = s23.enter_context(nc.psum_tensor("pb7b", [128, 1024], BF16))
        qFW2 = P.ctr("qFW2")
        qLG = P.ctr("qLG")
        qXG = P.ctr("qXG")
        qSC = P.ctr("qSC")
        qSD = [P.ctr(f"qSD{i}") for i in range(8)]
        qZ = P.ctr("qZ")
        qGA = P.ctr("qGA")
        qMT = P.ctr("qMT")
        qYS = P.ctr("qYS")
        qFN = P.ctr("qFN")
        TRASH = 5120.0
        with s23, ExitStack() as s2:
            EG = s2.enter_context
            xst2 = EG(nc.sbuf_tensor("xst2", [128, 4 * 512], F32))
            scr2 = EG(nc.sbuf_tensor("scr2", [128, 2 * 512], F32))
            xn2fs = EG(nc.sbuf_tensor("xn2fs", [128, 2 * 512], F32))
            xn2bst = EG(nc.sbuf_tensor("xn2bst", [128, 2 * 512], BF16))
            gsb = EG(nc.sbuf_tensor("gsb", [128, 64], F32))
            lg_sb = EG(nc.sbuf_tensor("lg_sb", [16, 2048], F32))
            mp1 = EG(nc.sbuf_tensor("mp1", [8, 2048], F32))
            mp2 = EG(nc.sbuf_tensor("mp2", [8, 2048], F32))
            lc1 = EG(nc.sbuf_tensor("lc1", [8, 2048], F32))
            lc2 = EG(nc.sbuf_tensor("lc2", [8, 2048], F32))
            tpl = EG(nc.sbuf_tensor("tpl", [8, 2048], F32))
            tp2 = EG(nc.sbuf_tensor("tp2", [8, 2048], F32))
            zpl8 = EG(nc.sbuf_tensor("zpl8", [8, 2048], F32))
            lgt = EG(nc.sbuf_tensor("lgt", [128, 16], F32))
            rt = EG(nc.sbuf_tensor("rt", [128, 96], F32))
            cnt = EG(nc.sbuf_tensor("cnt", [8, 32], F32))
            cmsb = EG(nc.sbuf_tensor("cmsb", [8, 16], F32))
            arena = EG(nc.sbuf_tensor("arena2", [48, 2048], F32))
            tmstg = EG(nc.sbuf_tensor("tmstg", [128, 4 * 1024], BF16))
            zrow = EG(nc.sbuf_tensor("zrow", [128, 512], F32))
            zrow2 = EG(nc.sbuf_tensor("zrow2", [128, 2048], F32))

            P.v(lambda e: e.memset(arena[:], 0.0))
            w1_rdy = None
            for dc in range(NDC):
                w1_rdy = P.dma("gpsimd", lambda e, dc=dc: e.dma_start(
                    out=w1s[:, dc * HID:(dc + 1) * HID],
                    in_=myw1[dc * 128:(dc + 1) * 128, :]), qFW)
            zi = P.v(lambda e: e.memset(zrow[:], 0.0))
            P.dma("gpsimd", lambda e: e.dma_start(
                out=swd.rearrange("(a p) c -> p a c", p=128),
                in_=zrow[:, 0:82].rearrange("p (a c) -> p a c", c=2)), qSC, [zi])
            zi2 = P.v(lambda e: e.memset(zrow2[:], 0.0))
            for i in range(64):
                P.dma("scalar", lambda e, i=i: e.dma_start(
                    out=ygl[i * 256:(i + 1) * 256, :].rearrange("(a p) c -> p a c", p=128),
                    in_=zrow2[:, 0:2048].rearrange("p (a c) -> p a c", c=1024)), qZ,
                    [zi2] if i == 0 else [])
            P.dma("sync", lambda e: e.dma_start(out=identf[:], in_=identf_t.ap()), qC)
            P.dma("sync", lambda e: e.dma_start(out=identb[:], in_=identb_t.ap()), qC)
            P.dma("sync", lambda e: e.dma_start(
                out=gsb[:, 0:64].rearrange("p (a e) -> p a e", e=8),
                in_=gw.rearrange("(a p) e -> p a e", p=128)), qC)
            P.dma("sync", lambda e: e.dma_start(out=cmsb[:, 0:8], in_=cmself), qC)
            P.dma("sync", lambda e: e.dma_start(out=cmsb[:, 8:16], in_=cmprev), qC)

            ln_stats(x1d, xst2, scr2, arena)
            P.barrier_all(exclude=(qFW, qSC))
            if _CKPT <= 1:
                P.flush()
                return nc

            slot_use = [None] * 4
            tm_guard = [None, None]
            lg_guard = [None, None]
            tm_store = [None] * 4
            for tb in range(NTB):
                bm, br_, bl = tb % 2, 2 + tb % 2, 4 + tb % 2
                P.mm(bm, lambda e, tb=tb, bm=bm: e.matmul(
                    pb[bm][:], onesC[0:1, :], arena[0:1, tb * TBW:(tb + 1) * TBW],
                    start=True, stop=True))
                P.mm(br_, lambda e, tb=tb, br_=br_: e.matmul(
                    pb[br_][:], onesC[32:33, :], arena[32:33, tb * TBW:(tb + 1) * TBW],
                    start=True, stop=True))
                for dc in range(NDC):
                    i = tb * NDC + dc
                    sl = i % 4
                    xsl = dc % 2
                    w = [slot_use[sl]] if slot_use[sl] else []
                    ld = P.dma("sync", lambda e, dc=dc, tb=tb, sl=sl: e.dma_start(
                        out=xst2[:, sl * 512:(sl + 1) * 512],
                        in_=x1d[dc, tb]), qX[sl], w)
                    P.rd("vector", bm, lambda e, sl=sl, bm=bm, xsl=xsl: e.tensor_tensor(
                        scr2[:, xsl * 512:(xsl + 1) * 512],
                        xst2[:, sl * 512:(sl + 1) * 512], pb[bm][:], ALU.subtract), [ld])
                    wg = [lg_guard[xsl]] if lg_guard[xsl] else []
                    mtok = P.rd("vector", br_, lambda e, xsl=xsl, br_=br_: e.tensor_tensor(
                        xn2fs[:, xsl * 512:(xsl + 1) * 512],
                        scr2[:, xsl * 512:(xsl + 1) * 512], pb[br_][:], ALU.mult), wg)
                    slot_use[sl] = mtok
                    ct = P.s(lambda e, xsl=xsl: e.copy(
                        xn2bst[:, xsl * 512:(xsl + 1) * 512],
                        xn2fs[:, xsl * 512:(xsl + 1) * 512]),
                        [mtok] + ([tm_guard[xsl]] if tm_guard[xsl] else []))
                    for j in range(4):
                        bkt = 6 + (j % 2)
                        P.mm(bkt, lambda e, xsl=xsl, j=j, bkt=bkt: e.transpose(
                            pb[bkt][:, 0:128],
                            xn2bst[:, xsl * 512 + j * 128: xsl * 512 + (j + 1) * 128],
                            identb[:]), [ct])
                        P.rd("vector", bkt, lambda e, dc=dc, j=j, bkt=bkt: e.tensor_copy(
                            tmstg[:, j * 1024 + dc * 128: j * 1024 + (dc + 1) * 128],
                            pb[bkt][:, 0:128]),
                            [tm_store[j]] if tm_store[j] else [])
                    tm_guard[xsl] = (P.mmS, P.mmS.n)
                    lg_guard[xsl] = P.mm(bl, lambda e, dc=dc, xsl=xsl, bl=bl: e.matmul(
                        pb[bl][0:8, :], gsb[:, dc * 8:(dc + 1) * 8],
                        xn2fs[:, xsl * 512:(xsl + 1) * 512],
                        start=(dc == 0), stop=(dc == NDC - 1)), [mtok, (qC, qC.n)])
                for j in range(4):
                    tm_store[j] = P.dma("sync", lambda e, tb=tb, j=j: e.dma_start(
                        out=xn2t[tb * TBW + j * 128: tb * TBW + (j + 1) * 128, :],
                        in_=tmstg[:, j * 1024:(j + 1) * 1024]),
                        qXG, [(P.veS, P.veS.n)])
                P.rd("vector", bl, lambda e, tb=tb, bl=bl: e.tensor_copy(
                    lg_sb[0:8, tb * TBW:(tb + 1) * TBW], pb[bl][0:8, :]))
            P.g(lambda e: e.collective_compute(
                "AllGather", ALU.bypass, replica_groups=[list(range(NCORE))],
                ins=[xn2t.opt()], outs=[xgall.opt()]),
                [(qXG, qXG.n), (ccS, 1)], [(ccS, 1)])
            P.barrier_all(exclude=(qFW, qFW2, qSC, ccS))
            tap("t_lg", lg_sb, 16, 2048, F32)
            tap("t_gsb", gsb, 128, 64, F32)
            tap("t_xn2t", xn2t, TPC, D, BF16)
            if _CKPT <= 2:
                P.barrier_all()
                P.flush()
                return nc

            P.v(lambda e: e.memset(rt[:], 0.0))
            for tt in range(16):
                P.mm(4, lambda e, tt=tt: e.transpose(
                    pb[4][:, 0:8], lg_sb[0:8, tt * 128:(tt + 1) * 128], identf[0:8, 0:8]))
                P.rd("vector", 4, lambda e: e.tensor_copy(lgt[:, 0:8], pb[4][:, 0:8]))
                P.v(lambda e: e.tensor_reduce(rt[:, 64:65], lgt[:, 0:8], AX.X, ALU.max))
                P.v(lambda e: e.tensor_scalar(rt[:, 0:8], lgt[:, 0:8], rt[:, 64:65], None, ALU.is_ge))
                P.v(lambda e: e.tensor_tensor_scan(rt[:, 8:16], rt[:, 0:8], rt[:, 80:88],
                                                   0.0, ALU.add, ALU.add))
                P.v(lambda e: e.tensor_scalar(rt[:, 16:24], rt[:, 8:16], 1.0, None, ALU.is_le))
                P.v(lambda e: e.tensor_tensor(rt[:, 24:32], rt[:, 0:8], rt[:, 16:24], ALU.mult))
                P.v(lambda e: e.scalar_tensor_tensor(rt[:, 32:40], rt[:, 24:32], -1e30,
                                                     lgt[:, 0:8], ALU.mult, ALU.add))
                P.v(lambda e: e.tensor_reduce(rt[:, 65:66], rt[:, 32:40], AX.X, ALU.max))
                P.v(lambda e: e.tensor_tensor(rt[:, 66:67], rt[:, 64:65], rt[:, 65:66], ALU.subtract))
                P.s(lambda e: e.activation(rt[:, 67:68], rt[:, 66:67], AF.Sigmoid), [(P.veS, P.veS.n)])
                sg = P.s(lambda e: e.activation(rt[:, 68:69], rt[:, 66:67], AF.Sigmoid, scale=-1.0))
                P.v(lambda e: e.tensor_scalar(rt[:, 40:48], rt[:, 32:40], rt[:, 65:66], None, ALU.is_ge))
                P.v(lambda e: e.tensor_tensor_scan(rt[:, 48:56], rt[:, 40:48], rt[:, 80:88],
                                                   0.0, ALU.add, ALU.add))
                P.v(lambda e: e.tensor_scalar(rt[:, 16:24], rt[:, 48:56], 1.0, None, ALU.is_le))
                P.v(lambda e: e.tensor_tensor(rt[:, 56:64], rt[:, 40:48], rt[:, 16:24], ALU.mult))
                P.v(lambda e: e.tensor_scalar(rt[:, 0:8], rt[:, 24:32], rt[:, 67:68], None, ALU.mult), [sg])
                P.v(lambda e: e.tensor_scalar(rt[:, 8:16], rt[:, 56:64], rt[:, 68:69], None, ALU.mult))
                P.mm(4, lambda e: e.transpose(pb[4][0:8, 0:128], rt[:, 0:8], identf[:]),
                     [(P.veS, P.veS.n)])
                P.rd("vector", 4, lambda e, tt=tt: e.tensor_copy(
                    mp1[0:8, tt * 128:(tt + 1) * 128], pb[4][0:8, 0:128]))
                P.mm(5, lambda e: e.transpose(pb[5][0:8, 0:128], rt[:, 8:16], identf[:]),
                     [(P.veS, P.veS.n)])
                P.rd("vector", 5, lambda e, tt=tt: e.tensor_copy(
                    mp2[0:8, tt * 128:(tt + 1) * 128], pb[5][0:8, 0:128]))

            zpl = zpl8[:]
            zm = P.v(lambda e: e.memset(zpl8[:], 0.0))
            P.v(lambda e: e.tensor_scalar(tpl[:], mp1[:], 0.0, None, ALU.is_gt))
            P.v(lambda e: e.tensor_tensor_scan(lc1[:], tpl[:], zpl, 0.0, ALU.add, ALU.add), [zm])
            P.v(lambda e: e.tensor_scalar(tpl[:], mp2[:], 0.0, None, ALU.is_gt))
            P.v(lambda e: e.tensor_tensor_scan(lc2[:], tpl[:], zpl, 0.0, ALU.add, ALU.add))
            P.v(lambda e: e.tensor_scalar(cnt[:, 0:8], cmsb[:, 0:8], lc1[:, 2047:2048], None, ALU.mult))
            ctok = P.v(lambda e: e.tensor_scalar(cnt[:, 8:16], cmsb[:, 0:8], lc2[:, 2047:2048],
                                                 None, ALU.mult))
            P.dma("gpsimd", lambda e: e.dma_start(out=ccin[0:8, :], in_=cnt[:, 0:8]), qCC, [ctok])
            P.dma("gpsimd", lambda e: e.dma_start(out=ccin[8:16, :], in_=cnt[:, 8:16]), qCC)
            P.g(lambda e: e.collective_compute(
                "AllReduce", ALU.add, replica_groups=[list(range(NCORE))],
                ins=[ccin.opt()], outs=[ccout.opt()]), [(qCC, qCC.n), (ccS, 2)], [(ccS, 1)])
            l1 = P.dma("gpsimd", lambda e: e.dma_start(out=cnt[:, 16:24], in_=ccout[0:8, :]), qCC, [(ccS, 3)])
            l2 = P.dma("gpsimd", lambda e: e.dma_start(out=cnt[:, 24:32], in_=ccout[8:16, :]), qCC)
            P.v(lambda e: e.tensor_tensor(rt[0:8, 0:8], cnt[:, 16:24], cmsb[:, 8:16], ALU.mult), [l1, l2])
            P.v(lambda e: e.tensor_reduce(rt[0:8, 64:65], rt[0:8, 0:8], AX.X, ALU.add))
            P.v(lambda e: e.tensor_reduce(rt[0:8, 65:66], cnt[:, 16:24], AX.X, ALU.add))
            P.v(lambda e: e.tensor_tensor(rt[0:8, 8:16], cnt[:, 24:32], cmsb[:, 8:16], ALU.mult))
            P.v(lambda e: e.tensor_reduce(rt[0:8, 66:67], rt[0:8, 8:16], AX.X, ALU.add))
            P.v(lambda e: e.tensor_tensor(rt[0:8, 66:67], rt[0:8, 66:67], rt[0:8, 65:66], ALU.add))
            if _CKPT <= 3:
                P.barrier_all()
                P.flush()
                return nc
            P.v(lambda e: e.tensor_scalar(tp2[:], mp1[:], 0.0, None, ALU.is_gt))
            P.v(lambda e: e.tensor_scalar(tpl[:], lc1[:], rt[0:8, 64:65], None, ALU.add))
            P.v(lambda e: e.tensor_scalar(lc1[:], tpl[:], CAP, None, ALU.is_le))
            P.v(lambda e: e.tensor_tensor(tp2[:], lc1[:], tp2[:], ALU.mult))
            P.v(lambda e: e.tensor_tensor(mp1[:], lc1[:], mp1[:], ALU.mult))
            P.v(lambda e: e.tensor_scalar(tpl[:], tpl[:], 1.0 + TRASH, None, ALU.subtract))
            P.v(lambda e: e.tensor_tensor(tpl[:], tp2[:], tpl[:], ALU.mult))
            P.v(lambda e: e.tensor_scalar(tp2[:], mp2[:], 0.0, None, ALU.is_gt))
            P.v(lambda e: e.tensor_scalar(lc2[:], lc2[:], rt[0:8, 66:67], None, ALU.add))
            P.v(lambda e: e.tensor_scalar(lc1[:], lc2[:], CAP, None, ALU.is_le))
            P.v(lambda e: e.tensor_tensor(tp2[:], lc1[:], tp2[:], ALU.mult))
            P.v(lambda e: e.tensor_tensor(mp2[:], lc1[:], mp2[:], ALU.mult))
            P.v(lambda e: e.tensor_scalar(lc2[:], lc2[:], 1.0 + TRASH, None, ALU.subtract))
            P.v(lambda e: e.tensor_tensor(lc2[:], tp2[:], lc2[:], ALU.mult))
            P.v(lambda e: e.tensor_tensor(tpl[:], tpl[:], lc2[:], ALU.add))
            P.v(lambda e: e.tensor_scalar(tpl[:], tpl[:], TRASH, None, ALU.add))
            sw1 = P.v(lambda e: e.tensor_tensor(mp1[:], mp1[:], mp2[:], ALU.add))
            d1 = P.dma("sync", lambda e: e.dma_start(out=swpd[0:8, :], in_=tpl[:]), qLG, [sw1])
            d2 = P.dma("sync", lambda e: e.dma_start(out=swpd[8:16, :], in_=mp1[:]), qLG)
            P.g(lambda e: e.collective_compute(
                "AllGather", ALU.bypass, replica_groups=[list(range(NCORE))],
                ins=[swpd.opt()], outs=[swall.opt()]),
                [(qLG, qLG.n), (ccS, 3)], [(ccS, 1)])
            mrw = EG(nc.sbuf_tensor("mrw", [16, 1], mybir.dt.int32))
            swc = EG(nc.sbuf_tensor("swc", [16, 2048], F32))
            pkT = EG(nc.sbuf_tensor("pkT", [128, 16], F32))
            pk2 = EG(nc.sbuf_tensor("pk2", [128, 2 * 8], F32))
            sxi = EG(nc.sbuf_tensor("sxi", [128, 8], mybir.dt.int32))
            gallsb = EG(nc.sbuf_tensor("gallsb", [128, 128], F32))
            mld = P.dma("sync", lambda e: e.dma_start(out=mrw[:], in_=myrows), qC)
            gld = P.dma("sync", lambda e: e.dma_start(out=gallsb[:], in_=gall_t.ap()), qC)
            swl = P.dma("gpsimd", lambda e: e.indirect_dma_start(
                swc[:], None, swall, IndirectOffsetOnAxis(mrw[:], 0)), qGA,
                [(ccS, 4), mld])
            tap("t_swall", swall, 128, 2048, F32)
            if _CKPT <= 35:
                P.barrier_all()
                P.flush()
                return nc
            sc_guards = [None] * 8
            for c_ in range(NCORE):
                for tt in range(16):
                    k = c_ * 16 + tt
                    sd = k % 8
                    P.mm(4, lambda e, c_=c_, tt=tt: e.transpose(
                        pb[4][:, 0:16], swc[0:16, tt * 128:(tt + 1) * 128],
                        identf[0:16, 0:16]), [swl])
                    P.rd("vector", 4, lambda e: e.tensor_copy(pkT[:], pb[4][:, 0:16]))
                    w0 = [sc_guards[sd]] if sc_guards[sd] else []
                    P.v(lambda e, c_=c_, sd=sd: e.tensor_copy(
                        sxi[:, sd:sd + 1], pkT[:, c_:c_ + 1]), w0)
                    P.v(lambda e, k=k, sd=sd: e.tensor_copy(
                        pk2[:, sd * 2:sd * 2 + 1], gallsb[:, k:k + 1]), [gld])
                    pv = P.v(lambda e, c_=c_, sd=sd: e.tensor_copy(
                        pk2[:, sd * 2 + 1:sd * 2 + 2], pkT[:, 8 + c_:9 + c_]))
                    sc_guards[sd] = P.dma("gpsimd", lambda e, sd=sd: e.indirect_dma_start(
                        swd, IndirectOffsetOnAxis(sxi[:, sd:sd + 1], 0),
                        pk2[:, sd * 2:sd * 2 + 2], None), qSD[sd],
                        [pv])
            if _CKPT <= 4:
                P.barrier_all()
                P.flush()
                return nc
            P.barrier_all(exclude=(qFW, qFW2))
            P.flush()
            s2.close()

            s3 = ExitStack()
            EH = s3.enter_context
            w2s = EH(nc.sbuf_tensor("w2s", [128, NHC * 1024], BF16))
            hbf = EH(nc.sbuf_tensor("hbf", [128, NHC * 512], BF16))
            xgf = EH(nc.sbuf_tensor("xgf", [128, 2 * NDC * 512], BF16))
            xgt = EH(nc.sbuf_tensor("xgt", [128, 2 * 1024], BF16))
            smt = EH(nc.sbuf_tensor("smt", [128, 8 * 2], F32))
            sxt = EH(nc.sbuf_tensor("sxt", [128, 8], mybir.dt.int32))
            yss = EH(nc.sbuf_tensor("yss", [128, 2 * 1024], F32))
            pjb = EH(nc.sbuf_tensor("pjb", [128, D], F32))

            w2_rdy = None
            for hc in range(NHC):
                w2_rdy = P.dma("gpsimd", lambda e, hc=hc: e.dma_start(
                    out=w2s[:, hc * 1024:(hc + 1) * 1024],
                    in_=myw2[hc * 128:(hc + 1) * 128, :]), qFW2)
            P.dma("sync", lambda e: e.dma_start(out=pjb[0:1, :], in_=mypjbr), qC)
            P.mm(4, lambda e: e.matmul(pb[4][:], onesC[0:1, :], pjb[0:1, 0:512],
                                       start=True, stop=True), [(qC, qC.n)])
            P.rd("scalar", 4, lambda e: e.copy(pjb[:, 0:512], pb[4][:]))
            P.mm(4, lambda e: e.matmul(pb[4][:], onesC[0:1, :], pjb[0:1, 512:1024],
                                       start=True, stop=True))
            pjr = P.rd("scalar", 4, lambda e: e.copy(pjb[:, 512:1024], pb[4][:]))

            NTS = 10
            xg_rdy = [None, None]
            meta_use = [None] * 4
            xgf_use = [None, None]
            gelu_guard = None
            hbf_guard = None
            yss_use = [None, None]
            for ts in range(NTS):
                sl = ts % 2
                for stt in range(4):
                    stile = ts * 4 + stt
                    w0 = [meta_use[stt]] if meta_use[stt] else []
                    md = P.dma("sync", lambda e, stile=stile, stt=stt: e.dma_start(
                        out=smt[:, stt * 2:(stt + 1) * 2],
                        in_=swd[stile * 128:(stile + 1) * 128, :]), qMT,
                        [(qSC, qSC.n)] + [(c, c.n) for c in qSD] + w0)
                    cv = P.v(lambda e, stt=stt: e.tensor_copy(
                        sxt[:, stt:stt + 1], smt[:, stt * 2: stt * 2 + 1]), [md])
                    gx = P.dma("gpsimd", lambda e, sl=sl, stt=stt: e.indirect_dma_start(
                        xgt[:, (stt % 2) * 1024:(stt % 2 + 1) * 1024], None,
                        xgall, IndirectOffsetOnAxis(sxt[:, stt:stt + 1], 0)), qGA,
                        [cv, (ccS, 2)] + ([xg_rdy[stt % 2]] if xg_rdy[stt % 2] else []))
                    for dc in range(NDC):
                        bkt = 6
                        P.mm(bkt, lambda e, sl=sl, stt=stt, dc=dc: e.transpose(
                            pb[bkt][:, 0:128],
                            xgt[:, (stt % 2) * 1024 + dc * 128: (stt % 2) * 1024 + (dc + 1) * 128],
                            identb[:]),
                            [gx] + ([xgf_use[sl]] if xgf_use[sl] else []))
                        P.rd("vector", bkt, lambda e, sl=sl, stt=stt, dc=dc: e.tensor_copy(
                            xgf[:, sl * 4096 + dc * 512 + stt * 128: sl * 4096 + dc * 512 + (stt + 1) * 128],
                            pb[bkt][:, 0:128]))
                    xg_rdy[stt % 2] = (P.mmS, P.mmS.n)
                xgf_rdy = (P.veS, P.veS.n)
                for hc in range(NHC):
                    bk = hc % 2
                    def mm1(e, hc=hc, sl=sl, bk=bk):
                        last = None
                        for dc in range(NDC):
                            last = e.matmul(
                                pb[bk][:],
                                w1s[:, dc * HID + hc * 128: dc * HID + (hc + 1) * 128],
                                xgf[:, sl * 4096 + dc * 512: sl * 4096 + (dc + 1) * 512],
                                start=(dc == 0), stop=(dc == NDC - 1))
                        return last
                    P.mm(bk, mm1, [w1_rdy, xgf_rdy])
                    P.rd("scalar", bk, lambda e, hc=hc, bk=bk: e.activation(
                        hbf[:, hc * 512:(hc + 1) * 512], pb[bk][:], AF.Gelu,
                        bias=fcbs[:, hc:hc + 1]),
                        [hbf_guard] if hbf_guard else [])
                xgf_use[sl] = (P.mmS, P.mmS.n)
                gelu_all = (P.scS, P.scS.n)
                for stt in range(4):
                    ysl = stt % 2
                    for half in range(2):
                        bk = 2 + half
                        def mm2(e, stt=stt, half=half, bk=bk):
                            last = None
                            for hc in range(NHC):
                                last = e.matmul(
                                    pb[bk][:],
                                    hbf[:, hc * 512 + stt * 128: hc * 512 + (stt + 1) * 128],
                                    w2s[:, hc * 1024 + half * 512: hc * 1024 + (half + 1) * 512],
                                    start=(hc == 0), stop=(hc == NHC - 1))
                            return last
                        P.mm(bk, mm2, [gelu_all, w2_rdy])
                        P.rd("vector", bk, lambda e, half=half, ysl=ysl, bk=bk: e.scalar_tensor_tensor(
                            yss[:, ysl * 1024 + half * 512: ysl * 1024 + (half + 1) * 512],
                            pb[bk][:], 1.0,
                            pjb[:, half * 512:(half + 1) * 512], ALU.mult, ALU.add),
                            [pjr] + ([yss_use[ysl]] if yss_use[ysl] else []))
                    ev = P.v(lambda e, stt=stt, ysl=ysl: e.tensor_scalar(
                        yss[:, ysl * 1024:(ysl + 1) * 1024],
                        yss[:, ysl * 1024:(ysl + 1) * 1024],
                        smt[:, stt * 2 + 1: stt * 2 + 2], None, ALU.mult))
                    yss_use[ysl] = P.dma("gpsimd", lambda e, stt=stt, ysl=ysl: e.indirect_dma_start(
                        ygl, IndirectOffsetOnAxis(sxt[:, stt:stt + 1], 0),
                        yss[:, ysl * 1024:(ysl + 1) * 1024], None,
                        compute_op=ALU.add), qYS,
                        [ev, (qZ, qZ.n)])
                    meta_use[stt] = yss_use[ysl]
                hbf_guard = (P.mmS, P.mmS.n)
            if _CKPT <= 5:
                P.barrier_all()
                P.flush()
                return nc
            P.g(lambda e: e.collective_compute(
                "ReduceScatter", ALU.add, replica_groups=[list(range(NCORE))],
                ins=[ygl.opt()], outs=[yred.opt()]),
                [(qYS, qYS.n), (ccS, 4)], [(ccS, 1)])
            P.barrier_all(exclude=())
            s3.close()
            P.flush()
            if _CKPT <= 6:
                return nc

            s4 = ExitStack()
            EF = s4.enter_context
            yt = EF(nc.sbuf_tensor("yt", [128, 2 * 1024], F32))
            fst = EF(nc.sbuf_tensor("fst", [128, 4 * 128], F32))
            ld_use = [None, None]
            fst_use = [None] * 4
            for tb in range(NTB):
                for j in range(4):
                    i = tb * 4 + j
                    sl = i % 2
                    w0 = [ld_use[sl]] if ld_use[sl] else []
                    ld = P.dma("sync", lambda e, tb=tb, j=j, sl=sl: e.dma_start(
                        out=yt[:, sl * 1024:(sl + 1) * 1024],
                        in_=yred[tb * TBW + j * 128: tb * TBW + (j + 1) * 128, :]),
                        qMT, w0)
                    toks = []
                    for dc in range(NDC):
                        fsl = dc % 4
                        bkt = dc % 2
                        P.mm(bkt, lambda e, sl=sl, dc=dc, bkt=bkt: e.transpose(
                            pb[bkt][:, 0:128],
                            yt[:, sl * 1024 + dc * 128: sl * 1024 + (dc + 1) * 128],
                            identf[:]), [ld])
                        fv = P.rd("vector", bkt, lambda e, fsl=fsl, bkt=bkt: e.tensor_copy(
                            fst[:, fsl * 128:(fsl + 1) * 128], pb[bkt][:, 0:128]),
                            [fst_use[fsl]] if fst_use[fsl] else [])
                        fst_use[fsl] = P.dma("gpsimd", lambda e, tb=tb, j=j, dc=dc, fsl=fsl: e.dma_start(
                            out=ysfm[dc, tb, :, j * 128:(j + 1) * 128],
                            in_=fst[:, fsl * 128:(fsl + 1) * 128],
                            accum_op=ALU.add), qFN, [fv])
                    ld_use[sl] = (P.mmS, P.mmS.n)
            P.emit("gpsimd", None, waits=[(qFN, qFN.n), (qST[0], qST[0].n),
                                          (qST[1], qST[1].n)])
            P.flush()
            s4.close()

    return nc


def kernel(L0_attn_x, L0_attn_Wq, L0_attn_bq, L0_attn_Wk, L0_attn_bk,
           L0_attn_Wv, L0_attn_bv, L0_attn_Wo, L0_attn_bo, L0_attn_P,
           L0_ffn_gate_w, L0_ffn_c_fc, L0_ffn_fc_bias, L0_ffn_c_proj,
           L0_ffn_proj_bias):
    x = np.ascontiguousarray(np.asarray(L0_attn_x, dtype=np.float32))
    B, N, D_ = x.shape
    X2 = x.reshape(B * N, D_)

    c = np.ascontiguousarray
    shared = dict(
        wqT=c(np.asarray(L0_attn_Wq, np.float32).T),
        wkT=c(np.asarray(L0_attn_Wk, np.float32).T),
        wvT=c(np.asarray(L0_attn_Wv, np.float32).T),
        woT=c(np.asarray(L0_attn_Wo, np.float32).T),
        bqkvo=c(np.concatenate(
            [np.asarray(b, np.float32).reshape(8, 128).T
             for b in (L0_attn_bq, L0_attn_bk, L0_attn_bv, L0_attn_bo)], axis=1)),
        bvrow=c(np.asarray(L0_attn_bv, np.float32).reshape(1, D_)),
        ptT=c(np.asarray(L0_attn_P, np.float32).T),
        gw=c(np.asarray(L0_ffn_gate_w, np.float32)),
    )
    cfc = np.asarray(L0_ffn_c_fc, np.float32)
    cpj = np.asarray(L0_ffn_c_proj, np.float32)
    fcb = np.asarray(L0_ffn_fc_bias, np.float32)
    pjbf = np.asarray(L0_ffn_proj_bias, np.float32)

    in_maps = []
    for core in range(NCORE):
        sl = X2[core * TPC:(core + 1) * TPC]
        xfm_ = c(sl.T.reshape(NDC, 128, NTB, TBW).transpose(0, 2, 1, 3))
        cms_ = np.zeros((8, 8), np.float32)
        cms_[:, core] = 1.0
        cmp_ = np.zeros((8, 8), np.float32)
        cmp_[:, :core] = 1.0
        myrows_ = np.array([16 * cc + core for cc in range(8)] +
                           [16 * cc + 8 + core for cc in range(8)],
                           np.int32).reshape(16, 1)
        m = dict(shared)
        m.update(xfm=xfm_, cmself=cms_, cmprev=cmp_,
                 myw1=c(cfc[core]),
                 myfcb=c(fcb[core].reshape(32, 128).T),
                 myw2=c(cpj[core]),
                 mypjbr=c(pjbf[core].reshape(1, 1024)),
                 myrows=myrows_)
        in_maps.append(m)

    nc = build_program()
    res = run_bass_kernel_spmd(nc, in_maps, core_ids=list(range(NCORE)))

    out = np.empty((B * N, D_), np.float32)
    for core in range(NCORE):
        yf = np.asarray(res.results[core]["ysfm"])
        out[core * TPC:(core + 1) * TPC] = yf.transpose(0, 2, 1, 3).reshape(D_, TPC).T
    return out.reshape(B, N, D_)


# version 20
# speedup vs baseline: 1.2530x; 1.2530x over previous
import numpy as np
import ml_dtypes
from contextlib import ExitStack
from types import SimpleNamespace

import concourse.bass as bass
import concourse.mybir as mybir
from concourse.bass_utils import run_bass_kernel_spmd

F32 = mybir.dt.float32
BF16 = mybir.dt.bfloat16
AF = mybir.ActivationFunctionType
ALU = mybir.AluOpType
AX = mybir.AxisListType

NCORE = 8
D = 1024
TPC = 2048
NTB = 4
TBW = 512
NDC = 8
NEXP = 8
HID = 4096
NHC = 32
CAP = 5120.0
EPS_LN = 1e-5
EPS_ATT = 1e-6
EPS_FEAT = 1e-6
PNORM = 1.0 / (64.0 ** 0.25)

ENGINES = ("sync", "gpsimd", "tensor", "vector", "scalar")

_TAPS = ()


class K:

    def __init__(self, nc, stk):
        self.nc = nc
        self.stk = stk
        self.seq = []
        self.ctrs = []
        self.veS = self.ctr("veS")
        self.scS = self.ctr("scS")
        self.mmS = self.ctr("mmS")
        self.bw = [None] * 8
        self.br = [[] for _ in range(8)]

    def ctr(self, name):
        h = self.stk.enter_context(self.nc.semaphore(name))
        c = SimpleNamespace(h=h, n=0, name=name)
        self.ctrs.append(c)
        return c

    def emit(self, eng, fn, waits=(), incs=()):
        seen = {}
        for c, v in waits:
            assert 0 <= v <= c.n, f"wait {c.name}>={v} but only {c.n} emitted"
            if v > 0:
                prev = seen.get(id(c))
                seen[id(c)] = (c.h, v if prev is None else max(v, prev[1]))
        ws = list(seen.values())
        rincs = []
        for c, k in incs:
            c.n += k
            rincs.append((c.h, k))
        self.seq.append((eng, ws, fn, rincs))

    def v(self, fn, waits=()):
        w = list(waits)
        if self.veS.n:
            w.append((self.veS, self.veS.n))
        self.emit("vector", fn, w, [(self.veS, 1)])
        return (self.veS, self.veS.n)

    def s(self, fn, waits=()):
        w = list(waits)
        if self.scS.n:
            w.append((self.scS, self.scS.n))
        self.emit("scalar", fn, w, [(self.scS, 1)])
        return (self.scS, self.scS.n)

    def dma(self, eng, fn, c, waits=()):
        self.emit(eng, fn, waits, [(c, 16)])
        return (c, c.n)

    def g(self, fn, waits=(), incs=()):
        self.emit("gpsimd", fn, waits, incs)

    def mm(self, bank, fn, waits=()):
        w = list(waits) + self.br[bank]
        if self.bw[bank] is not None:
            w.append(self.bw[bank])
        self.emit("tensor", fn, w, [(self.mmS, 1)])
        tok = (self.mmS, self.mmS.n)
        self.bw[bank] = tok
        self.br[bank] = []
        return tok

    def rd(self, eng, bank, fn, waits=()):
        w = list(waits)
        if self.bw[bank] is not None:
            w.append(self.bw[bank])
        tok = self.v(fn, w) if eng == "vector" else self.s(fn, w)
        self.br[bank].append(tok)
        return tok

    def snap(self):
        return [(c, c.n) for c in self.ctrs]

    def barrier_all(self, exclude=()):
        ex = set(id(c) for c in exclude)
        sn = [(c, v) for c, v in self.snap() if id(c) not in ex]
        for e in ENGINES:
            self.emit(e, None, waits=[(c, v) for c, v in sn if v > 0])
        self.bw = [None] * 8
        self.br = [[] for _ in range(8)]

    def flush(self):
        with self.nc.Block(no_gpsimd_drain=True) as block:
            for en in ENGINES:
                def run(engine, en=en):
                    for eng, ws, fn, incs in self.seq:
                        if eng != en:
                            continue
                        for h, v in ws:
                            engine.wait_ge(h, v)
                        inst = fn(engine) if fn is not None else None
                        for h, kk in incs:
                            inst.then_inc(h, kk)
                getattr(block, en)(run)
        self.seq = []


def build_program():
    nc = bass.Bass("TRN2", target_bir_lowering=False)

    def inp(name, shape, dt=F32):
        return nc.dram_tensor(name, list(shape), dt, kind="ExternalInput").ap()

    xfm = inp("xfm", (NDC, NTB, 128, TBW))
    wqT = inp("wqT", (D, D))
    wkT = inp("wkT", (D, D))
    wvT = inp("wvT", (D, D))
    woT = inp("woT", (D, D))
    bqkvo = inp("bqkvo", (128, 32))
    bvrow = inp("bvrow", (1, D))
    ptT = inp("ptT", (64, 64))
    gw = inp("gw", (D, NEXP))
    cfc = inp("cfc", (NEXP, D, HID))
    fcbi = inp("fcbi", (NEXP, 128, 32))
    cpj = inp("cpj", (NEXP, HID, D))
    pjbi = inp("pjbi", (NEXP, 128, 8))
    cmself = inp("cmself", (8, 8))
    cmprev = inp("cmprev", (8, 8))

    identf_t = nc.inline_tensor(np.eye(128, dtype=np.float32), "identc")
    selm = np.zeros((16, 2048), np.float32)
    for r_ in range(16):
        selm[r_, r_ * 128:(r_ + 1) * 128] = 1.0
    selm_t = nc.inline_tensor(selm, "selmc")

    def intern(name, shape, dt=F32):
        kind = "ExternalOutput" if name in _TAPS else "Internal"
        return nc.dram_tensor(name, list(shape), dt, kind=kind).ap()

    x1d = intern("x1d", (NDC, NTB, 128, TBW))
    xn2d = intern("xn2d", (NDC, NTB, 128, TBW), BF16)
    cpjb = intern("cpjb", (NEXP, HID, D), BF16)

    kvin = intern("kvin", (128, 520))
    kvout = intern("kvout", (128, 520))
    ccin = intern("ccin", (16, 8))
    ccout = intern("ccout", (16, 8))
    ysfm = nc.dram_tensor("ysfm", [NDC, NTB, 128, TBW], F32, kind="ExternalOutput").ap()


    stk = ExitStack()
    with stk:
        E = stk.enter_context
        P = K(nc, stk)

        qC = P.ctr("qC")
        qX = [P.ctr(f"qX{i}") for i in range(4)]
        qWT = P.ctr("qWT")
        qKV = P.ctr("qKV")
        qST = [P.ctr(f"qST{i}") for i in range(2)]
        qN = [P.ctr(f"qN{i}") for i in range(2)]
        qCC = P.ctr("qCC")
        qFW = P.ctr("qFW")
        qFX = [P.ctr(f"qFX{i}") for i in range(2)]
        qYP = [P.ctr(f"qYP{i}") for i in range(2)]
        qFS = [P.ctr(f"qFS{i}") for i in range(2)]
        qOUT = P.ctr("qOUT")
        ccS = P.ctr("ccS")
        qPC = P.ctr("qPC")
        qTAP = P.ctr("qTAP")

        def tap(name, buf, rows, cols, dt=F32):
            if name not in _TAPS:
                return
            t = nc.dram_tensor(name, [rows, cols], dt, kind="ExternalOutput").ap()
            P.barrier_all()
            P.dma("sync", lambda e: e.dma_start(out=t, in_=buf[0:rows, 0:cols]), qTAP)
            P.barrier_all()

        onesC = E(nc.sbuf_tensor("onesC", [128, 128], F32))
        selS = E(nc.sbuf_tensor("selS", [16, 2048], F32))
        w_sb = E(nc.sbuf_tensor("w_sb", [16, 2048], F32))
        fcbs = E(nc.sbuf_tensor("fcbs", [128, 256], F32))
        pjbs = E(nc.sbuf_tensor("pjbs", [128, 64], F32))
        pb = [E(nc.psum_tensor(f"pb{i}", [128, 512], F32)) for i in range(8)]

        P.dma("sync", lambda e: e.dma_start(out=selS[:], in_=selm_t.ap()), qC)
        for e_ in range(NEXP):
            P.dma("sync", lambda e, e_=e_: e.dma_start(out=fcbs[:, e_ * 32:(e_ + 1) * 32], in_=fcbi[e_]), qC)
            P.dma("sync", lambda e, e_=e_: e.dma_start(out=pjbs[:, e_ * 8:(e_ + 1) * 8], in_=pjbi[e_]), qC)
        P.v(lambda e: e.memset(w_sb[:], 0.0))
        P.v(lambda e: e.memset(onesC[:], 1.0))

        def ln_stats(src, xbuf, sqbuf, arena):
            slot_use = [None] * 4
            sq_use = [None, None]
            for dc in range(NDC):
                for tb in range(NTB):
                    i = dc * NTB + tb
                    sl = i % 4
                    w = [slot_use[sl]] if slot_use[sl] else []
                    ld = P.dma("sync", lambda e, dc=dc, tb=tb, sl=sl: e.dma_start(
                        out=xbuf[:, sl * 512:(sl + 1) * 512],
                        in_=src[dc, tb]), qX[sl], w)
                    sq = P.s(lambda e, sl=sl: e.activation(
                        sqbuf[:, (sl % 2) * 512:(sl % 2 + 1) * 512],
                        xbuf[:, sl * 512:(sl + 1) * 512], AF.Square),
                        [ld] + ([sq_use[sl % 2]] if sq_use[sl % 2] else []))
                    P.mm(tb, lambda e, tb=tb, sl=sl, dc=dc: e.matmul(
                        pb[tb][0:33, :], onesC[:, 0:33], xbuf[:, sl * 512:(sl + 1) * 512],
                        start=(dc == 0), stop=(dc == NDC - 1)), [ld])
                    tok = P.mm(4 + tb, lambda e, tb=tb, sl=sl, dc=dc: e.matmul(
                        pb[4 + tb][0:33, :], onesC[:, 0:33],
                        sqbuf[:, (sl % 2) * 512:(sl % 2 + 1) * 512],
                        start=(dc == 0), stop=(dc == NDC - 1)), [sq])
                    slot_use[sl] = tok
                    sq_use[sl % 2] = tok
            stats_done = (P.mmS, P.mmS.n)
            for tb in range(NTB):
                cl = slice(tb * TBW, (tb + 1) * TBW)
                P.rd("vector", tb, lambda e, tb=tb, cl=cl: e.tensor_scalar(
                    arena[0:1, cl], pb[tb][0:1, :], 1.0 / D, None, ALU.mult), [stats_done])
                P.rd("vector", tb, lambda e, tb=tb, cl=cl: e.tensor_scalar(
                    sqbuf[32:33, 0:512], pb[tb][32:33, :], 1.0 / D, None, ALU.mult))
                P.v(lambda e: e.tensor_tensor(sqbuf[32:33, 0:512], sqbuf[32:33, 0:512],
                                              sqbuf[32:33, 0:512], ALU.mult))
                P.rd("vector", 4 + tb, lambda e, tb=tb: e.tensor_scalar(
                    sqbuf[32:33, 512:1024], pb[4 + tb][32:33, :], 1.0 / D, EPS_LN,
                    ALU.mult, ALU.add))
                P.v(lambda e: e.tensor_tensor(sqbuf[32:33, 512:1024], sqbuf[32:33, 512:1024],
                                              sqbuf[32:33, 0:512], ALU.subtract))
                P.s(lambda e: e.activation(sqbuf[32:33, 0:512], sqbuf[32:33, 512:1024],
                                           AF.Sqrt), [(P.veS, P.veS.n)])
                P.v(lambda e, cl=cl: e.reciprocal(arena[32:33, cl], sqbuf[32:33, 0:512]),
                    [(P.scS, P.scS.n)])

        with ExitStack() as s1:
            EA = s1.enter_context
            xn = EA(nc.sbuf_tensor("xn", [128, NDC * TPC], BF16))
            qf = EA(nc.sbuf_tensor("qf", [128, NDC * TPC], BF16))
            kf = EA(nc.sbuf_tensor("kf", [128, 16 * 1024], BF16))
            vt = EA(nc.sbuf_tensor("vt", [128, 16 * 1024], BF16))
            qk = EA(nc.sbuf_tensor("qk", [128, TPC], BF16))
            xst = EA(nc.sbuf_tensor("xst", [128, 4 * 512], F32))
            wts = EA(nc.sbuf_tensor("wts", [128, 8 * 1024], BF16))
            scr = EA(nc.sbuf_tensor("scr", [128, 3 * 512], F32))
            rbsb = EA(nc.sbuf_tensor("rbsb", [128, 512], F32))
            stg = EA(nc.sbuf_tensor("stg", [128, 2 * 512], F32))
            kvpack = EA(nc.sbuf_tensor("kvpack", [128, 520], F32))
            kv_bf = EA(nc.sbuf_tensor("kv_bf", [128, 520], BF16))
            blkP = EA(nc.sbuf_tensor("blkP", [128, 128], BF16))
            bqs = EA(nc.sbuf_tensor("bqs", [128, 32], F32))
            bvb = EA(nc.sbuf_tensor("bvb", [128, D], F32))
            ptsb = EA(nc.sbuf_tensor("ptsb", [128, 64], F32))
            ones_bf = EA(nc.sbuf_tensor("ones_bf", [128, 1], BF16))
            arena = EA(nc.sbuf_tensor("arena", [48, 2048], F32))
            P.v(lambda e: e.memset(arena[:], 0.0))

            P.dma("sync", lambda e: e.dma_start(out=bqs[:], in_=bqkvo), qC)
            P.dma("sync", lambda e: e.dma_start(out=ptsb[0:64, :], in_=ptT), qC)
            P.dma("sync", lambda e: e.dma_start(out=ptsb[64:128, :], in_=ptT), qC)
            P.dma("sync", lambda e: e.dma_start(out=bvb[0:1, :], in_=bvrow), qC)
            cload = (qC, qC.n)
            P.v(lambda e: e.memset(ones_bf[:], 1.0))
            mz = P.v(lambda e: e.memset(blkP[:], 0.0))
            P.s(lambda e: e.activation(blkP[0:64, 0:64], ptsb[0:64, :], AF.Copy, scale=PNORM),
                [cload, mz])
            blkP_rdy = P.s(lambda e: e.activation(blkP[64:128, 64:128], ptsb[64:128, :],
                                                  AF.Copy, scale=PNORM), [cload])
            vz = (P.veS, P.veS.n)
            P.mm(6, lambda e: e.matmul(pb[6][:], onesC[0:1, :], bvb[0:1, 0:512],
                                       start=True, stop=True), [cload, vz])
            P.mm(7, lambda e: e.matmul(pb[7][:], onesC[0:1, :], bvb[0:1, 512:1024],
                                       start=True, stop=True), [cload, vz])
            P.rd("vector", 6, lambda e: e.tensor_copy(bvb[:, 0:512], pb[6][:]))
            bvb_rdy = P.rd("vector", 7, lambda e: e.tensor_copy(bvb[:, 512:1024], pb[7][:]))

            def load_wt(src, waits):
                last = None
                for dc in range(NDC):
                    last = P.dma("gpsimd", lambda e, dc=dc, src=src: e.dma_start(
                        out=wts[:, dc * 1024:(dc + 1) * 1024],
                        in_=src[dc * 128:(dc + 1) * 128, :]), qWT,
                        waits if dc == 0 else [])
                return last

            wq_rdy = load_wt(wqT, [])

            P.barrier_all()
            ln_stats(xfm, xst, scr, arena)
            P.barrier_all()
            tap("t_ar", arena, 48, 2048, F32)

            slot_use = [None] * 4
            for tb in range(NTB):
                bm, br_ = 4 + (tb % 2) * 2, 5 + (tb % 2) * 2
                P.mm(bm, lambda e, tb=tb, bm=bm: e.matmul(
                    pb[bm][:], onesC[0:1, :], arena[0:1, tb * TBW:(tb + 1) * TBW],
                    start=True, stop=True))
                P.mm(br_, lambda e, tb=tb, br_=br_: e.matmul(
                    pb[br_][:], onesC[32:33, :], arena[32:33, tb * TBW:(tb + 1) * TBW],
                    start=True, stop=True))
                for dc in range(NDC):
                    i = tb * NDC + dc
                    sl = i % 4
                    w = [slot_use[sl]] if slot_use[sl] else []
                    ld = P.dma("sync", lambda e, dc=dc, tb=tb, sl=sl: e.dma_start(
                        out=xst[:, sl * 512:(sl + 1) * 512],
                        in_=xfm[dc, tb]), qX[sl], w)
                    P.rd("vector", bm, lambda e, sl=sl, bm=bm: e.tensor_tensor(
                        scr[:, (sl % 2) * 512:(sl % 2 + 1) * 512],
                        xst[:, sl * 512:(sl + 1) * 512], pb[bm][:], ALU.subtract), [ld])
                    tok = P.rd("vector", br_, lambda e, sl=sl, dc=dc, tb=tb, br_=br_: e.tensor_tensor(
                        xn[:, dc * TPC + tb * TBW: dc * TPC + (tb + 1) * TBW],
                        scr[:, (sl % 2) * 512:(sl % 2 + 1) * 512], pb[br_][:], ALU.mult))
                    slot_use[sl] = tok
            P.barrier_all()
            tap("t_xn", xn, 128, NDC * TPC, BF16)

            qk_guard = None

            def proj(which, w_rdy, bias_off):
                nonlocal qk_guard
                for oc in range(NDC):
                    banks = [0, 1, 2, 3] if oc % 2 == 0 else [4, 5, 6, 7]
                    for tb in range(NTB):
                        bk = banks[tb]
                        def mmg2(e, oc=oc, tb=tb, bk=bk):
                            last = None
                            for dc in range(NDC):
                                last = e.matmul(
                                    pb[bk][:],
                                    wts[:, dc * 1024 + oc * 128: dc * 1024 + (oc + 1) * 128],
                                    xn[:, dc * TPC + tb * TBW: dc * TPC + (tb + 1) * TBW],
                                    start=(dc == 0), stop=(dc == NDC - 1))
                            return last
                        P.mm(bk, mmg2, [w_rdy])
                    for tb in range(NTB):
                        bk = banks[tb]
                        w = [qk_guard] if qk_guard else []
                        P.rd("scalar", bk, lambda e, oc=oc, tb=tb, bk=bk: e.add(
                            qk[:, tb * TBW:(tb + 1) * TBW], pb[bk][:],
                            bqs[:, bias_off + oc: bias_off + oc + 1]), w)
                    if which == "q":
                        for tb in range(NTB):
                            bk = banks[tb]
                            tok = P.mm(bk, lambda e, tb=tb, bk=bk: e.matmul(
                                pb[bk][:], blkP[:], qk[:, tb * TBW:(tb + 1) * TBW],
                                start=True, stop=True), [blkP_rdy])
                            P.rd("vector", bk, lambda e, oc=oc, tb=tb, bk=bk: e.tensor_scalar(
                                qf[:, oc * TPC + tb * TBW: oc * TPC + (tb + 1) * TBW],
                                pb[bk][:], 0.0, EPS_FEAT, ALU.max, ALU.add))
                        qk_guard = tok
                    else:
                        for tb4 in range(NTB):
                            bk = banks[tb4]
                            def mmk(e, oc=oc, tb4=tb4, bk=bk):
                                last = None
                                for j in range(4):
                                    tc = tb4 * 4 + j
                                    last = e.matmul(
                                        pb[bk][:, j * 128:(j + 1) * 128],
                                        qk[:, tc * 128:(tc + 1) * 128], blkP[:],
                                        start=True, stop=True)
                                return last
                            tok = P.mm(bk, mmk, [blkP_rdy])
                            def evk(e, oc=oc, tb4=tb4, bk=bk):
                                last = None
                                for j in range(4):
                                    tc = tb4 * 4 + j
                                    last = e.tensor_scalar(
                                        kf[:, tc * 1024 + oc * 128: tc * 1024 + (oc + 1) * 128],
                                        pb[bk][:, j * 128:(j + 1) * 128], 0.0, EPS_FEAT,
                                        ALU.max, ALU.add)
                                return last
                            P.rd("vector", bk, evk)
                        qk_guard = tok

            proj("q", wq_rdy, 0)
            wk_rdy = load_wt(wkT, [(P.mmS, P.mmS.n)])
            proj("k", wk_rdy, 8)
            wv_rdy = load_wt(wvT, [(P.mmS, P.mmS.n)])

            for tc in range(16):
                for half in range(2):
                    bk = 4 * (tc % 2) + half
                    def mmv(e, tc=tc, half=half, bk=bk):
                        last = None
                        for dc in range(NDC):
                            last = e.matmul(
                                pb[bk][:],
                                xn[:, dc * TPC + tc * 128: dc * TPC + (tc + 1) * 128],
                                wts[:, dc * 1024 + half * 512: dc * 1024 + (half + 1) * 512],
                                start=(dc == 0), stop=(dc == NDC - 1))
                        return last
                    P.mm(bk, mmv, [wv_rdy])
                    P.rd("vector", bk, lambda e, tc=tc, half=half, bk=bk: e.scalar_tensor_tensor(
                        vt[:, tc * 1024 + half * 512: tc * 1024 + (half + 1) * 512],
                        pb[bk][:], 1.0, bvb[:, half * 512:(half + 1) * 512],
                        ALU.mult, ALU.add), [bvb_rdy])
            wo_rdy = load_wt(woT, [(P.mmS, P.mmS.n)])
            pc_tok = []
            for e_ in range(NEXP):
                pc_tok.append(P.dma("gpsimd", lambda e, e_=e_: e.dma_start(
                    out=cpjb[e_], in_=cpj[e_]), qPC))
            P.barrier_all()
            tap("t_qf", qf, 128, NDC * TPC, BF16)
            tap("t_kf", kf, 128, 16 * 1024, BF16)
            tap("t_vt", vt, 128, 16 * 1024, BF16)

            for hp in range(NDC):
                bk = hp % 2
                for par in range(2):
                    base = par * 64
                    h_ = hp * 2 + par
                    def mmkv(e, h_=h_, base=base, bk=bk):
                        last = None
                        for tc in range(16):
                            last = e.matmul(
                                pb[bk][base:base + 64, 0:64],
                                kf[:, tc * 1024 + h_ * 64: tc * 1024 + (h_ + 1) * 64],
                                vt[:, tc * 1024 + h_ * 64: tc * 1024 + (h_ + 1) * 64],
                                start=(tc == 0), stop=(tc == 15))
                        return last
                    P.mm(bk, mmkv)
                def mmks(e, hp=hp):
                    last = None
                    for tc in range(16):
                        last = e.matmul(
                            pb[2][:, hp:hp + 1],
                            kf[:, tc * 1024 + hp * 128: tc * 1024 + (hp + 1) * 128],
                            ones_bf[:], start=(tc == 0), stop=(tc == 15))
                    return last
                P.mm(2, mmks)
                P.rd("vector", bk, lambda e, hp=hp, bk=bk: e.tensor_copy(
                    kvpack[:, hp * 65: hp * 65 + 64], pb[bk][:, 0:64]))
            P.rd("vector", 2, lambda e: e.tensor_copy(kvpack[:, 64:520:65], pb[2][:, 0:8]))
            kv_done = (P.veS, P.veS.n)
            P.dma("gpsimd", lambda e: e.dma_start(out=kvin, in_=kvpack[:]), qKV, [kv_done])
            P.g(lambda e: e.collective_compute(
                "AllReduce", ALU.add, replica_groups=[[0, 1], [2, 3], [4, 5], [6, 7]],
                ins=[kvin.opt()], outs=[kvout.opt()]), [(qKV, qKV.n)], [(ccS, 1)])
            kvld = P.dma("gpsimd", lambda e: e.dma_start(out=kvpack[:], in_=kvout), qKV, [(ccS, 1)])
            kvbf_rdy = P.v(lambda e: e.tensor_copy(kv_bf[:], kvpack[:]), [kvld])
            P.barrier_all()
            tap("t_kv", kvpack, 128, 520, F32)

            rden_guard = None
            rbsE_guard = None
            rbsO_guard = None
            for hp in range(NDC):
                for tb in range(NTB):
                    i = hp * NTB + tb
                    bkE = i % 2
                    bkO = 2 + i % 2
                    bkD = 4 + i % 2
                    c0 = hp * TPC + tb * TBW
                    P.mm(bkE, lambda e, hp=hp, c0=c0, bkE=bkE: e.matmul(
                        pb[bkE][0:65, :], kv_bf[0:64, hp * 65:(hp + 1) * 65],
                        qf[0:64, c0:c0 + TBW], start=True, stop=True))
                    P.mm(bkO, lambda e, hp=hp, c0=c0, bkO=bkO: e.matmul(
                        pb[bkO][64:128, :], kv_bf[64:128, hp * 65: hp * 65 + 64],
                        qf[64:128, c0:c0 + TBW], start=True, stop=True))
                    P.mm(bkD, lambda e, hp=hp, c0=c0, bkD=bkD: e.matmul(
                        pb[bkD][64:65, :], kv_bf[64:128, hp * 65 + 64: hp * 65 + 65],
                        qf[64:128, c0:c0 + TBW], start=True, stop=True))
                    wrd = [rden_guard] if rden_guard else []
                    P.rd("vector", bkE, lambda e, bkE=bkE: e.tensor_scalar(
                        scr[64:65, 0:512], pb[bkE][64:65, :], EPS_ATT, None, ALU.add), wrd)
                    P.v(lambda e: e.reciprocal(scr[64:65, 0:512], scr[64:65, 0:512]))
                    P.rd("vector", bkD, lambda e, bkD=bkD: e.tensor_scalar(
                        scr[64:65, 512:1024], pb[bkD][64:65, :], EPS_ATT, None, ALU.add))
                    rcp = P.v(lambda e: e.reciprocal(scr[64:65, 512:1024], scr[64:65, 512:1024]))
                    bE = P.mm(6, lambda e: e.matmul(
                        pb[6][0:64, :], onesC[64:65, 0:64], scr[64:65, 0:512],
                        start=True, stop=True), [rcp])
                    bO = P.mm(7, lambda e: e.matmul(
                        pb[7][64:128, :], onesC[64:65, 0:64], scr[64:65, 512:1024],
                        start=True, stop=True), [rcp])
                    rden_guard = bO
                    rbsE = P.rd("scalar", 6, lambda e: e.copy(rbsb[0:64, :], pb[6][0:64, :]),
                                [rbsE_guard] if rbsE_guard else [])
                    rbsO = P.rd("scalar", 7, lambda e: e.copy(rbsb[64:128, :], pb[7][64:128, :]),
                                [rbsO_guard] if rbsO_guard else [])
                    rbsE_guard = P.rd("vector", bkE, lambda e, c0=c0, bkE=bkE: e.tensor_tensor(
                        qf[0:64, c0:c0 + TBW], pb[bkE][0:64, :], rbsb[0:64, :],
                        ALU.mult), [rbsE])
                    rbsO_guard = P.rd("vector", bkO, lambda e, c0=c0, bkO=bkO: e.tensor_tensor(
                        qf[64:128, c0:c0 + TBW], pb[bkO][64:128, :], rbsb[64:128, :],
                        ALU.mult), [rbsO])
            P.barrier_all()
            tap("t_att", qf, 128, NDC * TPC, BF16)

            slot_use = [None] * 4
            store_use = [None, None]
            for oc in range(NDC):
                banks = [0, 1, 2, 3] if oc % 2 == 0 else [4, 5, 6, 7]
                for tb in range(NTB):
                    bk = banks[tb]
                    def mmo(e, oc=oc, tb=tb, bk=bk):
                        last = None
                        for dc in range(NDC):
                            last = e.matmul(
                                pb[bk][:],
                                wts[:, dc * 1024 + oc * 128: dc * 1024 + (oc + 1) * 128],
                                qf[:, dc * TPC + tb * TBW: dc * TPC + (tb + 1) * TBW],
                                start=(dc == 0), stop=(dc == NDC - 1))
                        return last
                    P.mm(bk, mmo, [wo_rdy])
                for tb in range(NTB):
                    bk = banks[tb]
                    i = oc * NTB + tb
                    sl = i % 4
                    ssl = i % 2
                    w = [slot_use[sl]] if slot_use[sl] else []
                    ld = P.dma("sync", lambda e, oc=oc, tb=tb, sl=sl: e.dma_start(
                        out=xst[:, sl * 512:(sl + 1) * 512],
                        in_=xfm[oc, tb]), qX[sl], w)
                    wst = [ld] + ([store_use[ssl]] if store_use[ssl] else [])
                    ev = P.rd("vector", bk, lambda e, oc=oc, sl=sl, ssl=ssl, bk=bk: e.scalar_tensor_tensor(
                        stg[:, ssl * 512:(ssl + 1) * 512], pb[bk][:],
                        bqs[:, 24 + oc: 25 + oc], xst[:, sl * 512:(sl + 1) * 512],
                        ALU.add, ALU.add), wst)
                    slot_use[sl] = ev
                    P.dma("sync", lambda e, oc=oc, tb=tb, ssl=ssl: e.dma_start(
                        out=x1d[oc, tb],
                        in_=stg[:, ssl * 512:(ssl + 1) * 512]), qST[ssl], [ev])
                    store_use[ssl] = P.dma("sync", lambda e, oc=oc, tb=tb, ssl=ssl: e.dma_start(
                        out=ysfm[oc, tb],
                        in_=stg[:, ssl * 512:(ssl + 1) * 512]), qST[ssl], [])
            P.barrier_all()
            P.flush()

        s23 = ExitStack()
        w1s = s23.enter_context(nc.sbuf_tensor("w1s", [128, NDC * HID], BF16))
        qFW2 = P.ctr("qFW2")
        with s23, ExitStack() as s2:
            EG = s2.enter_context
            xst2 = EG(nc.sbuf_tensor("xst2", [128, 4 * 512], F32))
            scr2 = EG(nc.sbuf_tensor("scr2", [128, 2 * 512], F32))
            xn2fs = EG(nc.sbuf_tensor("xn2fs", [128, 2 * 512], F32))
            xn2bst = EG(nc.sbuf_tensor("xn2bst", [128, 2 * 512], BF16))
            gsb = EG(nc.sbuf_tensor("gsb", [128, 64], F32))
            lg_sb = EG(nc.sbuf_tensor("lg_sb", [16, 2048], F32))
            mp1 = EG(nc.sbuf_tensor("mp1", [8, 2048], F32))
            mp2 = EG(nc.sbuf_tensor("mp2", [8, 2048], F32))
            lc1 = EG(nc.sbuf_tensor("lc1", [8, 2048], F32))
            lc2 = EG(nc.sbuf_tensor("lc2", [8, 2048], F32))
            tpl = EG(nc.sbuf_tensor("tpl", [8, 2048], F32))
            zz = EG(nc.sbuf_tensor("zz", [8, 2048], F32))
            lgt = EG(nc.sbuf_tensor("lgt", [128, 16], F32))
            rt = EG(nc.sbuf_tensor("rt", [128, 96], F32))
            cnt = EG(nc.sbuf_tensor("cnt", [8, 32], F32))
            cmsb = EG(nc.sbuf_tensor("cmsb", [8, 16], F32))
            arena = EG(nc.sbuf_tensor("arena2", [48, 2048], F32))
            identf = EG(nc.sbuf_tensor("identf", [128, 128], F32))

            P.v(lambda e: e.memset(arena[:], 0.0))
            w1_rdy0 = None
            for dc in range(NDC):
                w1_rdy0 = P.dma("gpsimd", lambda e, dc=dc: e.dma_start(
                    out=w1s[:, dc * HID:(dc + 1) * HID],
                    in_=cfc[0, dc * 128:(dc + 1) * 128, :]), qFW)
            P.dma("sync", lambda e: e.dma_start(out=identf[:], in_=identf_t.ap()), qC)
            P.dma("sync", lambda e: e.dma_start(
                out=gsb[:, 0:64].rearrange("p (a e) -> p a e", e=8),
                in_=gw.rearrange("(a p) e -> p a e", p=128)), qC)
            P.dma("sync", lambda e: e.dma_start(out=cmsb[:, 0:8], in_=cmself), qC)
            P.dma("sync", lambda e: e.dma_start(out=cmsb[:, 8:16], in_=cmprev), qC)

            ln_stats(x1d, xst2, scr2, arena)
            P.barrier_all()

            slot_use = [None] * 4
            nstore_use = [None, None]
            lg_guard = [None, None]
            for tb in range(NTB):
                bm, br_, bl = tb % 2, 2 + tb % 2, 4 + tb % 2
                P.mm(bm, lambda e, tb=tb, bm=bm: e.matmul(
                    pb[bm][:], onesC[0:1, :], arena[0:1, tb * TBW:(tb + 1) * TBW],
                    start=True, stop=True))
                P.mm(br_, lambda e, tb=tb, br_=br_: e.matmul(
                    pb[br_][:], onesC[32:33, :], arena[32:33, tb * TBW:(tb + 1) * TBW],
                    start=True, stop=True))
                for dc in range(NDC):
                    i = tb * NDC + dc
                    sl = i % 4
                    xsl = dc % 2
                    w = [slot_use[sl]] if slot_use[sl] else []
                    ld = P.dma("sync", lambda e, dc=dc, tb=tb, sl=sl: e.dma_start(
                        out=xst2[:, sl * 512:(sl + 1) * 512],
                        in_=x1d[dc, tb]), qX[sl], w)
                    P.rd("vector", bm, lambda e, sl=sl, bm=bm, xsl=xsl: e.tensor_tensor(
                        scr2[:, xsl * 512:(xsl + 1) * 512],
                        xst2[:, sl * 512:(sl + 1) * 512], pb[bm][:], ALU.subtract), [ld])
                    wg = [lg_guard[xsl]] if lg_guard[xsl] else []
                    mtok = P.rd("vector", br_, lambda e, xsl=xsl, br_=br_: e.tensor_tensor(
                        xn2fs[:, xsl * 512:(xsl + 1) * 512],
                        scr2[:, xsl * 512:(xsl + 1) * 512], pb[br_][:], ALU.mult), wg)
                    slot_use[sl] = mtok
                    ct = P.s(lambda e, xsl=xsl: e.copy(
                        xn2bst[:, xsl * 512:(xsl + 1) * 512],
                        xn2fs[:, xsl * 512:(xsl + 1) * 512]),
                        [mtok] + ([nstore_use[xsl]] if nstore_use[xsl] else []))
                    nstore_use[xsl] = P.dma("sync", lambda e, dc=dc, tb=tb, xsl=xsl: e.dma_start(
                        out=xn2d[dc, tb],
                        in_=xn2bst[:, xsl * 512:(xsl + 1) * 512]), qN[xsl], [ct])
                    lg_guard[xsl] = P.mm(bl, lambda e, dc=dc, xsl=xsl, bl=bl: e.matmul(
                        pb[bl][0:8, :], gsb[:, dc * 8:(dc + 1) * 8],
                        xn2fs[:, xsl * 512:(xsl + 1) * 512],
                        start=(dc == 0), stop=(dc == NDC - 1)), [mtok, (qC, qC.n)])
                P.rd("vector", bl, lambda e, tb=tb, bl=bl: e.tensor_copy(
                    lg_sb[0:8, tb * TBW:(tb + 1) * TBW], pb[bl][0:8, :]))
            P.barrier_all()

            P.v(lambda e: e.memset(zz[:], 0.0))
            P.v(lambda e: e.memset(rt[:], 0.0))
            for tt in range(16):
                P.mm(6, lambda e, tt=tt: e.transpose(
                    pb[6][:, 0:8], lg_sb[0:8, tt * 128:(tt + 1) * 128], identf[0:8, 0:8]))
                P.rd("vector", 6, lambda e: e.tensor_copy(lgt[:, 0:8], pb[6][:, 0:8]))
                P.v(lambda e: e.tensor_reduce(rt[:, 64:65], lgt[:, 0:8], AX.X, ALU.max))
                P.v(lambda e: e.tensor_scalar(rt[:, 0:8], lgt[:, 0:8], rt[:, 64:65], None, ALU.is_ge))
                P.v(lambda e: e.tensor_tensor_scan(rt[:, 8:16], rt[:, 0:8], rt[:, 80:88],
                                                   0.0, ALU.add, ALU.add))
                P.v(lambda e: e.tensor_scalar(rt[:, 16:24], rt[:, 8:16], 1.0, None, ALU.is_le))
                P.v(lambda e: e.tensor_tensor(rt[:, 24:32], rt[:, 0:8], rt[:, 16:24], ALU.mult))
                P.v(lambda e: e.scalar_tensor_tensor(rt[:, 32:40], rt[:, 24:32], -1e30,
                                                     lgt[:, 0:8], ALU.mult, ALU.add))
                P.v(lambda e: e.tensor_reduce(rt[:, 65:66], rt[:, 32:40], AX.X, ALU.max))
                P.v(lambda e: e.tensor_tensor(rt[:, 66:67], rt[:, 64:65], rt[:, 65:66], ALU.subtract))
                P.s(lambda e: e.activation(rt[:, 67:68], rt[:, 66:67], AF.Sigmoid), [(P.veS, P.veS.n)])
                sg = P.s(lambda e: e.activation(rt[:, 68:69], rt[:, 66:67], AF.Sigmoid, scale=-1.0))
                P.v(lambda e: e.tensor_scalar(rt[:, 40:48], rt[:, 32:40], rt[:, 65:66], None, ALU.is_ge))
                P.v(lambda e: e.tensor_tensor_scan(rt[:, 48:56], rt[:, 40:48], rt[:, 80:88],
                                                   0.0, ALU.add, ALU.add))
                P.v(lambda e: e.tensor_scalar(rt[:, 16:24], rt[:, 48:56], 1.0, None, ALU.is_le))
                P.v(lambda e: e.tensor_tensor(rt[:, 56:64], rt[:, 40:48], rt[:, 16:24], ALU.mult))
                P.v(lambda e: e.tensor_scalar(rt[:, 0:8], rt[:, 24:32], rt[:, 67:68], None, ALU.mult), [sg])
                P.v(lambda e: e.tensor_scalar(rt[:, 8:16], rt[:, 56:64], rt[:, 68:69], None, ALU.mult))
                P.mm(6, lambda e: e.transpose(pb[6][0:8, 0:128], rt[:, 0:8], identf[:]),
                     [(P.veS, P.veS.n)])
                P.rd("vector", 6, lambda e, tt=tt: e.tensor_copy(
                    mp1[0:8, tt * 128:(tt + 1) * 128], pb[6][0:8, 0:128]))
                P.mm(7, lambda e: e.transpose(pb[7][0:8, 0:128], rt[:, 8:16], identf[:]),
                     [(P.veS, P.veS.n)])
                P.rd("vector", 7, lambda e, tt=tt: e.tensor_copy(
                    mp2[0:8, tt * 128:(tt + 1) * 128], pb[7][0:8, 0:128]))

            P.v(lambda e: e.tensor_scalar(tpl[:], mp1[:], 0.0, None, ALU.is_gt))
            P.v(lambda e: e.tensor_tensor_scan(lc1[:], tpl[:], zz[:], 0.0, ALU.add, ALU.add))
            P.v(lambda e: e.tensor_scalar(tpl[:], mp2[:], 0.0, None, ALU.is_gt))
            P.v(lambda e: e.tensor_tensor_scan(lc2[:], tpl[:], zz[:], 0.0, ALU.add, ALU.add))
            P.v(lambda e: e.tensor_scalar(cnt[:, 0:8], cmsb[:, 0:8], lc1[:, 2047:2048], None, ALU.mult))
            ctok = P.v(lambda e: e.tensor_scalar(cnt[:, 8:16], cmsb[:, 0:8], lc2[:, 2047:2048],
                                                 None, ALU.mult))
            P.dma("gpsimd", lambda e: e.dma_start(out=ccin[0:8, :], in_=cnt[:, 0:8]), qCC, [ctok])
            P.dma("gpsimd", lambda e: e.dma_start(out=ccin[8:16, :], in_=cnt[:, 8:16]), qCC)
            P.g(lambda e: e.collective_compute(
                "AllReduce", ALU.add, replica_groups=[list(range(NCORE))],
                ins=[ccin.opt()], outs=[ccout.opt()]), [(qCC, qCC.n), (ccS, 1)], [(ccS, 1)])
            l1 = P.dma("gpsimd", lambda e: e.dma_start(out=cnt[:, 16:24], in_=ccout[0:8, :]), qCC, [(ccS, 2)])
            l2 = P.dma("gpsimd", lambda e: e.dma_start(out=cnt[:, 24:32], in_=ccout[8:16, :]), qCC)
            P.v(lambda e: e.tensor_tensor(rt[0:8, 0:8], cnt[:, 16:24], cmsb[:, 8:16], ALU.mult), [l1, l2])
            P.v(lambda e: e.tensor_reduce(rt[0:8, 64:65], rt[0:8, 0:8], AX.X, ALU.add))
            P.v(lambda e: e.tensor_reduce(rt[0:8, 65:66], cnt[:, 16:24], AX.X, ALU.add))
            P.v(lambda e: e.tensor_tensor(rt[0:8, 8:16], cnt[:, 24:32], cmsb[:, 8:16], ALU.mult))
            P.v(lambda e: e.tensor_reduce(rt[0:8, 66:67], rt[0:8, 8:16], AX.X, ALU.add))
            P.v(lambda e: e.tensor_tensor(rt[0:8, 66:67], rt[0:8, 66:67], rt[0:8, 65:66], ALU.add))
            P.v(lambda e: e.tensor_scalar(tpl[:], lc1[:], rt[0:8, 64:65], None, ALU.add))
            P.v(lambda e: e.tensor_scalar(lc1[:], tpl[:], CAP, None, ALU.is_le))
            P.v(lambda e: e.tensor_tensor(mp1[:], lc1[:], mp1[:], ALU.mult))
            P.v(lambda e: e.tensor_scalar(tpl[:], lc2[:], rt[0:8, 66:67], None, ALU.add))
            P.v(lambda e: e.tensor_scalar(lc2[:], tpl[:], CAP, None, ALU.is_le))
            P.v(lambda e: e.tensor_tensor(mp2[:], lc2[:], mp2[:], ALU.mult))
            P.v(lambda e: e.tensor_tensor(w_sb[0:8, :], mp1[:], mp2[:], ALU.add))
            P.barrier_all(exclude=(qFW, qFW2))
            P.flush()
            s2.close()

            s3 = ExitStack()
            EH = s3.enter_context
            w2s = EH(nc.sbuf_tensor("w2s", [128, NHC * 1024], BF16))
            hbf = EH(nc.sbuf_tensor("hbf", [128, NHC * 512], BF16))
            xts = EH(nc.sbuf_tensor("xts", [128, 2 * NDC * 512], BF16))
            wbs = EH(nc.sbuf_tensor("wbs", [128, 2 * 512], F32))
            yst = EH(nc.sbuf_tensor("yst", [128, 2 * 512], F32))

            xts_use = [None, None]
            yst_use = [None, None]
            wbs_use = [None, None]
            p1_done = None
            p2_done = None
            hbf_guard = None
            w1_next = w1_rdy0

            def load_w2(ex, waits):
                tok = None
                for hc in range(NHC):
                    tok = P.dma("scalar", lambda e, ex=ex, hc=hc: e.dma_start(
                        out=w2s[:, hc * 1024:(hc + 1) * 1024],
                        in_=cpjb[ex, hc * 128:(hc + 1) * 128, :]), qFW2,
                        (waits + [pc_tok[ex]]) if hc == 0 else [])
                return tok

            def load_w1(ex, waits):
                tok = None
                for dc in range(NDC):
                    tok = P.dma("gpsimd", lambda e, ex=ex, dc=dc: e.dma_start(
                        out=w1s[:, dc * HID:(dc + 1) * HID],
                        in_=cfc[ex, dc * 128:(dc + 1) * 128, :]), qFW,
                        waits if dc == 0 else [])
                return tok

            for ex in range(NEXP):
                w1_rdy = w1_next
                w2_rdy = load_w2(ex, [p2_done] if p2_done else [])
                for ts in range(NTB):
                    sl = (ex * NTB + ts) % 2
                    ldx = None
                    for dc in range(NDC):
                        ldx = P.dma("sync", lambda e, ts=ts, dc=dc, sl=sl: e.dma_start(
                            out=xts[:, sl * 4096 + dc * 512: sl * 4096 + (dc + 1) * 512],
                            in_=xn2d[dc, ts]), qFX[sl],
                            ([xts_use[sl]] if xts_use[sl] and dc == 0 else []))
                    wb = P.mm(4 + sl, lambda e, ex=ex, ts=ts, sl=sl: e.matmul(
                        pb[4 + sl][:], selS[:, ex * 128:(ex + 1) * 128],
                        w_sb[:, ts * TBW:(ts + 1) * TBW], start=True, stop=True),
                        [wbs_use[sl]] if wbs_use[sl] else [])
                    wbr = P.rd("scalar", 4 + sl, lambda e, sl=sl: e.copy(
                        wbs[:, sl * 512:(sl + 1) * 512], pb[4 + sl][:]))
                    for hc in range(NHC):
                        bk = hc % 2
                        def mm1(e, hc=hc, sl=sl, bk=bk):
                            last = None
                            for dc in range(NDC):
                                last = e.matmul(
                                    pb[bk][:],
                                    w1s[:, dc * HID + hc * 128: dc * HID + (hc + 1) * 128],
                                    xts[:, sl * 4096 + dc * 512: sl * 4096 + (dc + 1) * 512],
                                    start=(dc == 0), stop=(dc == NDC - 1))
                            return last
                        P.mm(bk, mm1, [w1_rdy, ldx])
                        P.rd("scalar", bk, lambda e, ex=ex, hc=hc, bk=bk: e.activation(
                            hbf[:, hc * 512:(hc + 1) * 512], pb[bk][:], AF.Gelu,
                            bias=fcbs[:, ex * 32 + hc: ex * 32 + hc + 1]),
                            [hbf_guard] if hbf_guard else [])
                    gelu_all = (P.scS, P.scS.n)
                    if ts == NTB - 1:
                        p1_done = (P.mmS, P.mmS.n)
                        if ex + 1 < NEXP:
                            w1_next = load_w1(ex + 1, [p1_done])
                    for dc in range(NDC):
                        bk = 2 + dc % 2
                        def mm2(e, dc=dc, bk=bk):
                            last = None
                            for hc in range(NHC):
                                last = e.matmul(
                                    pb[bk][:],
                                    w2s[:, hc * 1024 + dc * 128: hc * 1024 + (dc + 1) * 128],
                                    hbf[:, hc * 512:(hc + 1) * 512],
                                    start=(hc == 0), stop=(hc == NHC - 1))
                            return last
                        P.mm(bk, mm2, [gelu_all, w2_rdy])
                        ysl = dc % 2
                        ev = P.rd("vector", bk, lambda e, ex=ex, dc=dc, ysl=ysl, sl=sl, bk=bk: e.scalar_tensor_tensor(
                            yst[:, ysl * 512:(ysl + 1) * 512], pb[bk][:],
                            pjbs[:, ex * 8 + dc: ex * 8 + dc + 1],
                            wbs[:, sl * 512:(sl + 1) * 512], ALU.add, ALU.mult),
                            [wbr] + ([yst_use[ysl]] if yst_use[ysl] else []))
                        yst_use[ysl] = P.dma("gpsimd", lambda e, ex=ex, dc=dc, ts=ts, ysl=ysl: e.dma_start(
                            out=ysfm[dc, ts],
                            in_=yst[:, ysl * 512:(ysl + 1) * 512],
                            accum_op=ALU.add), qYP[ysl], [ev])
                    hbf_guard = (P.mmS, P.mmS.n)
                    xts_use[sl] = (P.mmS, P.mmS.n)
                    wbs_use[sl] = (P.veS, P.veS.n)
                p2_done = (P.mmS, P.mmS.n)
            P.barrier_all()
            s3.close()
            P.flush()

        P.emit("gpsimd", None, waits=[(qYP[0], qYP[0].n), (qYP[1], qYP[1].n),
                                      (qST[0], qST[0].n), (qST[1], qST[1].n)])
        P.flush()

    return nc


def kernel(L0_attn_x, L0_attn_Wq, L0_attn_bq, L0_attn_Wk, L0_attn_bk,
           L0_attn_Wv, L0_attn_bv, L0_attn_Wo, L0_attn_bo, L0_attn_P,
           L0_ffn_gate_w, L0_ffn_c_fc, L0_ffn_fc_bias, L0_ffn_c_proj,
           L0_ffn_proj_bias):
    x = np.ascontiguousarray(np.asarray(L0_attn_x, dtype=np.float32))
    B, N, D_ = x.shape
    X2 = x.reshape(B * N, D_)

    c = np.ascontiguousarray
    shared = dict(
        wqT=c(np.asarray(L0_attn_Wq, np.float32).T),
        wkT=c(np.asarray(L0_attn_Wk, np.float32).T),
        wvT=c(np.asarray(L0_attn_Wv, np.float32).T),
        woT=c(np.asarray(L0_attn_Wo, np.float32).T),
        bqkvo=c(np.concatenate(
            [np.asarray(b, np.float32).reshape(8, 128).T
             for b in (L0_attn_bq, L0_attn_bk, L0_attn_bv, L0_attn_bo)], axis=1)),
        bvrow=c(np.asarray(L0_attn_bv, np.float32).reshape(1, D_)),
        ptT=c(np.asarray(L0_attn_P, np.float32).T),
        gw=c(np.asarray(L0_ffn_gate_w, np.float32)),
        cfc=c(np.asarray(L0_ffn_c_fc, np.float32)),
        fcbi=c(np.stack([np.asarray(L0_ffn_fc_bias, np.float32)[e].reshape(32, 128).T
                         for e in range(NEXP)])),
        cpj=c(np.asarray(L0_ffn_c_proj, np.float32)),
        pjbi=c(np.stack([np.asarray(L0_ffn_proj_bias, np.float32)[e].reshape(8, 128).T
                         for e in range(NEXP)])),
    )

    in_maps = []
    for core in range(NCORE):
        sl = X2[core * TPC:(core + 1) * TPC]
        xfm_ = c(sl.T.reshape(NDC, 128, NTB, TBW).transpose(0, 2, 1, 3))
        cms_ = np.zeros((8, 8), np.float32)
        cms_[:, core] = 1.0
        cmp_ = np.zeros((8, 8), np.float32)
        cmp_[:, :core] = 1.0
        m = dict(shared)
        m.update(xfm=xfm_, cmself=cms_, cmprev=cmp_)
        in_maps.append(m)

    nc = build_program()
    res = run_bass_kernel_spmd(nc, in_maps, core_ids=list(range(NCORE)))

    out = np.empty((B * N, D_), np.float32)
    for core in range(NCORE):
        yf = np.asarray(res.results[core]["ysfm"])
        out[core * TPC:(core + 1) * TPC] = yf.transpose(0, 2, 1, 3).reshape(D_, TPC).T
    return out.reshape(B, N, D_)
